# version 14
# speedup vs baseline: 1.0609x; 1.0158x over previous
import numpy as np
import ml_dtypes
import concourse.bass as bass
import concourse.mybir as mybir
from concourse.bass_utils import run_bass_kernel_spmd

F32 = mybir.dt.float32
BF16 = mybir.dt.bfloat16
AF = mybir.ActivationFunctionType
ALU = mybir.AluOpType
AX = mybir.AxisListType

D = 1024
T = 2048
NCORE = 8
DCH = 8
FF = 4096
FCH = 32
LBIG = float(2.0 ** 80)
CONST_VALS = [0.0, 1.0, -1.0, 0.5, -0.5, 1e-8, 1e-5, 1e-6, 1.0 + 1e-6, 128.5, -128.0,
              127.0, 1.0 / 127.0, 1.0 / 1024.0, 1.0 / 4096.0, 8.0, 0.125, -0.125]
CONST_VALS += [-LBIG * float(2 ** k) for k in range(8)] + [LBIG, -LBIG, -LBIG * 0.5]

DBG = []


class PB:

    ENGINES = ("tensor", "vector", "scalar", "sync", "gpsimd")

    def __init__(self, nc):
        self.nc = nc
        self.q = {e: [] for e in self.ENGINES}
        self.cnt = {"qS": 0, "qG": 0, "qW0": 0, "qW1": 0, "qX0": 0, "qX1": 0,
                    "eT": 0, "eV": 0, "eS": 0, "cc": 0}
        self.sems = {}
        self.last = {"tensor": None, "vector": None, "scalar": None}
        self.marks = {e: [] for e in self.ENGINES}

    def phase(self, name):
        for e in self.ENGINES:
            self.marks[e].append((len(self.q[e]), name))

    def op(self, eng, fn):
        self.q[eng].append(fn)

    def opm(self, eng, fn):
        sem = {"tensor": "eT", "vector": "eV", "scalar": "eS"}[eng]
        self.cnt[sem] += 1
        tgt = self.cnt[sem]
        def f(e, fn=fn, sem=sem):
            inst = fn(e)
            assert inst is not None, "opm fn must return its last instruction"
            inst.then_inc(self.sems[sem], 1)
        self.q[eng].append(f)
        self.last[eng] = (sem, tgt)
        return (sem, tgt)

    def dma(self, queue, out, in_, sem=None):
        if sem is None:
            sem = "qS" if queue == "sync" else "qG"
        self.cnt[sem] += 16
        tgt = self.cnt[sem]
        def f(e, out=out, in_=in_, sem=sem):
            e.dma_start(out=out, in_=in_).then_inc(self.sems[sem], 16)
        self.q[queue].append(f)
        return (sem, tgt)


    def sync_coarse(self, eng="sync"):
        self.wait(eng, ("eT", self.cnt["eT"]), ("eV", self.cnt["eV"]), ("eS", self.cnt["eS"]))

    def wait(self, eng, *tokens):
        for tok in tokens:
            if tok is None:
                continue
            sem, val = tok
            def f(e, sem=sem, val=val):
                e.wait_ge(self.sems[sem], val)
            self.q[eng].append(f)

    def barrier(self):
        toks = [t for t in self.last.values() if t is not None]
        for s in ("qS", "qG", "qW0", "qW1", "qX0", "qX1"):
            if self.cnt[s]:
                toks.append((s, self.cnt[s]))
        for eng in self.ENGINES:
            self.wait(eng, *toks)
        return toks

    def emit(self, block):
        for eng in self.ENGINES:
            ops = self.q[eng]
            if not ops:
                continue
            marks = self.marks[eng]
            def section(e, ops=ops, marks=marks):
                mi = 0
                cur = None
                for i, f in enumerate(ops):
                    while mi < len(marks) and marks[mi][0] <= i:
                        if cur is not None:
                            self.nc.leave_named_scope(cur[0], cur[1], False)
                        sid, _ = self.nc.enter_named_scope(marks[mi][1], False)
                        cur = (marks[mi][1], sid)
                        mi += 1
                    f(e)
                if cur is not None:
                    self.nc.leave_named_scope(cur[0], cur[1], False)
            getattr(block, eng)(section)


def build_program(n_cores=8, my_experts=2):
    nc = bass.Bass("TRN2", target_bir_lowering=False)
    b = PB(nc)
    g = {}

    g["x_h"] = nc.dram_tensor("x", [D, T], F32, kind="ExternalInput").ap()
    g["out_h"] = nc.dram_tensor("out", [D, T], F32, kind="ExternalOutput").ap()
    for nm, shape in [("W_inT", [D, D]), ("W_igT", [D, D]), ("W_rgT", [D, D]),
                      ("W_outT", [D, D]), ("w1T", [D, FF]), ("w2T", [FF, D]),
                      ("Wq", [D, D]), ("Wk", [D, D]), ("Wv", [D, D]), ("Wo", [D, D]),
                      ("dWq", [D, D]), ("dWk", [D, D]), ("dWv", [D, D]), ("dWo", [D, D]),
                      ("cWq", [D, D]), ("cWk", [D, D]), ("cWv", [D, D]), ("cWo", [D, D]),
                      ("eW1", [my_experts * D, FF]), ("eWg", [my_experts * D, FF]),
                      ("eW2", [my_experts * FF, D]),
                      ("conv_w", [128, 4 * FCH]), ("l0vecs", [128, 5 * DCH]),
                      ("l1vecs", [128, 16 * DCH]), ("eb", [128, 6 * DCH + 4 * FCH]),
                      ("slotq", [128, DCH * 16]),
                      ("ident", [128, 128]), ("onesc", [128, 2]), ("onesr", [1, 128]),
                      ("triu", [128, 128]), ("constp", [128, len(CONST_VALS)]),
                      ("dbv_row", [1, D]), ("cbv_row", [1, D]),
                      ("ebrow", [1, 2 * 9216]),
                      ("p1t", [128, 16]), ("p2t", [128, 16])]:
        g[nm + "_h"] = nc.dram_tensor(nm, shape, F32, kind="ExternalInput").ap()

    g["m1q_h"] = nc.dram_tensor("m1q", [D, FF], BF16, kind="Internal").ap()
    g["m2q_h"] = nc.dram_tensor("m2q", [FF, D], BF16, kind="Internal").ap()
    g["cc_in1"] = nc.dram_tensor("cc_in1", [16, D], F32, kind="Internal").ap()
    g["cc_out1"] = nc.dram_tensor("cc_out1", [128, D], F32, kind="Internal").ap()
    g["cc_in2"] = nc.dram_tensor("cc_in2", [16, D], F32, kind="Internal").ap()
    g["cc_out2"] = nc.dram_tensor("cc_out2", [128, D], F32, kind="Internal").ap()
    g["kh_scr"] = nc.dram_tensor("kh_scr", [D, T], F32, kind="Internal").ap()
    g["vh_scr"] = nc.dram_tensor("vh_scr", [T, D], F32, kind="Internal").ap()
    g["wspl_scr"] = nc.dram_tensor("wspl_scr", [3 * D, 2048], BF16, kind="Internal").ap()
    g["cwq_scr"] = nc.dram_tensor("cwq_scr", [D, 2048], BF16, kind="Internal").ap()
    g["l0w_scr"] = nc.dram_tensor("l0w_scr", [4 * D, 2048], BF16, kind="Internal").ap()
    g["dw_scr"] = nc.dram_tensor("dw_scr", [2 * D, 2048], BF16, kind="Internal").ap()

    g["X"] = nc.alloc_sbuf_tensor("X", [128, DCH * T], F32).ap()
    g["WF"] = nc.alloc_sbuf_tensor("WF", [128, 20480], F32).ap()
    g["WB"] = nc.alloc_sbuf_tensor("WB", [128, 24576], BF16).ap()
    g["CN"] = nc.alloc_sbuf_tensor("CN", [128, 1024], F32).ap()
    g["RW"] = nc.alloc_sbuf_tensor("RW", [128, 2048], F32).ap()
    g["PS"] = [nc.alloc_psum_tensor(f"PS{i}", [128, 512], F32).ap() for i in range(8)]
    g["CP"] = nc.alloc_sbuf_tensor("CP", [128, len(CONST_VALS)], F32).ap()
    for i, v in enumerate(CONST_VALS):
        nc.const_aps.register(F32, float(v), g["CP"][:, i:i + 1])

    for nm, (shape, dt) in dict((d[0], (d[1], d[2])) for d in DBG).items():
        g["dbg_" + nm] = nc.dram_tensor("dbg_" + nm, shape, dt, kind="ExternalOutput").ap()

    ctx = []
    for n in ("qS", "qG", "qW0", "qW1", "qX0", "qX1", "eT", "eV", "eS", "cc"):
        cm = nc.semaphore(n)
        b.sems[n] = cm.__enter__()
        ctx.append(cm)
    blk_cm = nc.Block()
    block = blk_cm.__enter__()
    try:
        program(b, g, n_cores, my_experts)
        b.emit(block)
    finally:
        blk_cm.__exit__(None, None, None)
        for cm in reversed(ctx):
            cm.__exit__(None, None, None)
    return nc



def ln_stats(b, g, src_ap, chunk_off, ts, TS, mu_row, inv_row, sq_buf, pbank, eps, ready):
    PS, CN = g["PS"], g["CN"]
    ONES = CN[:, 128:129]
    nj = (TS + 511) // 512
    b.wait("tensor", *ready)
    def t1(e):
        inst = None
        for j in range(nj):
            w = min(512, TS - j * 512)
            for c in range(DCH):
                inst = e.matmul(PS[pbank][0:1, j * 512:j * 512 + w], ONES,
                                src_ap[:, chunk_off(c) + ts + j * 512: chunk_off(c) + ts + j * 512 + w],
                                start=(c == 0), stop=(c == DCH - 1))
        return inst
    tsum = b.opm("tensor", t1)

    b.wait("scalar", *ready)
    def s1(e):
        for c in range(DCH):
            e.activation(sq_buf[:, c * TS:(c + 1) * TS],
                         src_ap[:, chunk_off(c) + ts: chunk_off(c) + ts + TS], AF.Square)
    b.op("scalar", s1)
    b.wait("scalar", tsum)
    def s2(e):
        return e.activation(mu_row, PS[pbank][0:1, 0:TS], AF.Copy, scale=1.0 / D)
    ssq = b.opm("scalar", s2)

    b.wait("tensor", ssq)
    def t2(e):
        inst = None
        for j in range(nj):
            w = min(512, TS - j * 512)
            for c in range(DCH):
                inst = e.matmul(PS[pbank][0:1, j * 512:j * 512 + w], ONES,
                                sq_buf[:, c * TS + j * 512: c * TS + j * 512 + w],
                                start=(c == 0), stop=(c == DCH - 1))
        return inst
    tsq = b.opm("tensor", t2)

    b.wait("vector", tsq)
    def v1a(e):
        return e.tensor_tensor(inv_row, mu_row, mu_row, op=ALU.mult)
    va = b.opm("vector", v1a)
    b.wait("vector", va)
    def v1(e):
        return e.scalar_tensor_tensor(inv_row, PS[pbank][0:1, 0:TS], 1.0 / D,
                                      inv_row, op0=ALU.mult, op1=ALU.subtract)
    vv = b.opm("vector", v1)
    b.wait("scalar", vv)
    def s3(e):
        return e.activation(inv_row, inv_row, AF.Sqrt, bias=eps)
    s3m = b.opm("scalar", s3)
    b.wait("vector", s3m)
    def v2(e):
        return e.reciprocal(inv_row, inv_row)
    return b.opm("vector", v2)


def bcast_rows(b, g, rows_ts, pbank_list, ready):
    PS, CN = g["PS"], g["CN"]
    ONESR = g["onesr_sb"]
    b.wait("tensor", *ready)
    def t(e):
        inst = None
        for (row, TS), pb in zip(rows_ts, pbank_list):
            for j in range((TS + 511) // 512):
                w = min(512, TS - j * 512)
                inst = e.matmul(PS[pb][:, j * 512:j * 512 + w], ONESR,
                                row[0:1, j * 512:j * 512 + w],
                                start=True, stop=True)
        return inst
    return b.opm("tensor", t)


def round8(b, g, y, w0, wk, ck, n_dst, ready):
    b.wait("scalar", *ready)
    def s0(e):
        return e.activation(w0, y, AF.Copy, bias=128.5)
    m = b.opm("scalar", s0)
    b.wait("scalar", m)
    def s7(e):
        return e.activation(ck, w0, AF.Sigmoid, scale=LBIG, bias=-LBIG * 128.0)
    m = b.opm("scalar", s7)
    b.wait("vector", m)
    def v7(e):
        return e.scalar_tensor_tensor(wk, ck, 128.0, w0, op0=ALU.mult, op1=ALU.subtract)
    mv = b.opm("vector", v7)
    for k in range(6, -1, -1):
        p = float(2 ** k)
        b.wait("scalar", mv)
        def sk(e, p=p):
            return e.activation(ck, wk, AF.Sigmoid, scale=-LBIG, bias=-LBIG * p)
        m = b.opm("scalar", sk)
        b.wait("vector", m)
        def vk(e, p=p):
            return e.scalar_tensor_tensor(wk, ck, p, wk, op0=ALU.mult, op1=ALU.add)
        mv = b.opm("vector", vk)
    b.wait("vector", mv)
    def vf(e):
        return e.scalar_tensor_tensor(n_dst, w0, -128.0, wk, op0=ALU.add, op1=ALU.add)
    return b.opm("vector", vf)


def round8_multi(b, g, streams, ready):
    b.wait("scalar", *ready)
    ms = []
    for st in streams:
        def s0(e, st=st):
            return e.activation(st[1], st[0], AF.Copy, bias=128.5)
        ms.append(b.opm("scalar", s0))
    mvs = [None] * len(streams)
    for i, st in enumerate(streams):
        b.wait("scalar", ms[i])
        def s7(e, st=st):
            return e.activation(st[3], st[1], AF.Sigmoid, scale=LBIG, bias=-LBIG * 128.0)
        ms[i] = b.opm("scalar", s7)
    for i, st in enumerate(streams):
        b.wait("vector", ms[i])
        def v7(e, st=st):
            return e.scalar_tensor_tensor(st[2], st[3], 128.0, st[1],
                                          op0=ALU.mult, op1=ALU.subtract)
        mvs[i] = b.opm("vector", v7)
    for k in range(6, -1, -1):
        p = float(2 ** k)
        for i, st in enumerate(streams):
            b.wait("scalar", mvs[i])
            def sk(e, st=st, p=p):
                return e.activation(st[3], st[2], AF.Sigmoid, scale=-LBIG, bias=-LBIG * p)
            ms[i] = b.opm("scalar", sk)
        for i, st in enumerate(streams):
            b.wait("vector", ms[i])
            def vk(e, st=st, p=p):
                return e.scalar_tensor_tensor(st[2], st[3], p, st[2],
                                              op0=ALU.mult, op1=ALU.add)
            mvs[i] = b.opm("vector", vk)
    outs = []
    for i, st in enumerate(streams):
        b.wait("vector", mvs[i])
        def vf(e, st=st):
            return e.scalar_tensor_tensor(st[4], st[1], -128.0, st[2],
                                          op0=ALU.add, op1=ALU.add)
        outs.append(b.opm("vector", vf))
    return outs


def cross_max_row(b, g, src, W, dst_row, col_sb, pbank, ready, op=None):
    PS, CN = g["PS"], g["CN"]
    IDENT = CN[:, 0:128]
    pos = 0
    mv = None
    while pos < W:
        w = min(128, W - pos)
        b.wait("tensor", *(ready if pos == 0 else [mv]))
        def tt(e, pos=pos, w=w):
            return e.transpose(PS[pbank][0:w, 0:128], src[:, pos:pos + w], IDENT)
        tm = b.opm("tensor", tt)
        b.wait("vector", tm)
        def vr(e, pos=pos, w=w):
            if op is None:
                return e.reduce_max(col_sb[0:w, 0:1], PS[pbank][0:w, 0:128], axis=AX.X)
            return e.tensor_reduce(col_sb[0:w, 0:1], PS[pbank][0:w, 0:128], axis=AX.X, op=op)
        vm = b.opm("vector", vr)
        b.wait("tensor", vm)
        def tt2(e, pos=pos, w=w):
            return e.transpose(PS[pbank + 1][0:1, 0:w], col_sb[0:w, 0:1], IDENT[0:w, 0:w])
        tm2 = b.opm("tensor", tt2)
        b.wait("scalar", tm2)
        def sc(e, pos=pos, w=w):
            return e.copy(dst_row[0:1, pos:pos + w], PS[pbank + 1][0:1, 0:w])
        mv = b.opm("scalar", sc)
        pos += w
    return mv


def program(b, g, n_cores, my_experts):
    nc = b.nc
    X, WF, WB, CN, RW, PS = g["X"], g["WF"], g["WB"], g["CN"], g["RW"], g["PS"]

    IDENT = CN[:, 0:128]
    ONES = CN[:, 128:129]
    B_IG = CN[:, 130:138]
    B_RG = CN[:, 138:146]
    LAM = CN[:, 146:154]
    NORMW = CN[:, 154:162]
    H0 = CN[:, 162:170]
    LOGA8 = CN[:, 170:178]
    HST = CN[:, 178:186]
    g["onesr_sb"] = CN[0:1, 186:314]
    SCAL = CN[:, 314:330]
    CONVW = CN[:, 336:336 + 4 * FCH]

    b.phase("p0")
    for c in range(DCH):
        b.dma("sync", X[:, c * T:(c + 1) * T], g["x_h"][c * 128:(c + 1) * 128, :])
    b.dma("sync", IDENT, g["ident_h"][:, :])
    b.dma("sync", CN[:, 128:130], g["onesc_h"][:, :])
    b.dma("sync", CN[0:1, 186:314], g["onesr_h"][:, :])
    b.dma("sync", CN[:, 130:170], g["l0vecs_h"][:, :])
    b.dma("sync", CONVW, g["conv_w_h"][:, :])
    b.dma("sync", CN[:, 464:592], g["triu_h"][:, :])
    p0 = b.dma("sync", g["CP"], g["constp_h"][:, :])

    b.wait("scalar", p0)
    def s0(e):
        return e.activation(LOGA8, LAM, AF.Sigmoid)
    m = b.opm("scalar", s0)
    b.wait("scalar", m)
    def s1(e):
        return e.activation(LOGA8, LOGA8, AF.Ln, bias=1e-8)
    m = b.opm("scalar", s1)
    b.wait("vector", m, p0)
    def v0(e):
        e.tensor_scalar_mul(LOGA8, LOGA8, 8.0)
        return e.tensor_copy(HST, H0)
    b.opm("vector", v0)
    b.barrier()

    b.phase("l0attn")
    TS = 512
    XLN, IBUF, RBUF, HBUF, WSP, TMP = 0, 4096, 8192, 12288, 16384, 18432
    XH, XL, WHI, WLO, HNH, HNL = 0, 4096, 8192, 10240, 12288, 16384

    lastmm = [None, None]
    spl0 = [None, None]
    for si in range(T // TS):
        ts = si * TS
        MU = WF[0:1, TMP:TMP + TS]
        INV = WF[0:1, TMP + 512:TMP + 512 + TS]
        RMS = WF[0:1, TMP + 1024:TMP + 1024 + TS]
        sb = [("eV", b.cnt["eV"]), ("eS", b.cnt["eS"]), ("eT", b.cnt["eT"])]
        lnm = ln_stats(b, g, X, lambda c: c * T, ts, TS, MU, INV,
                       WF[:, XLN:XLN + TS * DCH], 6, 1e-5, sb)
        bm = bcast_rows(b, g, [(MU, TS), (INV, TS)], [6, 7], [lnm])
        b.wait("vector", bm)
        def vna(e, ts=ts):
            inst = None
            for c in range(DCH):
                inst = e.tensor_tensor(WF[:, XLN + c * TS:XLN + (c + 1) * TS],
                                       X[:, c * T + ts:c * T + ts + TS], PS[6][:, 0:TS],
                                       op=ALU.subtract)
            return inst
        vma = b.opm("vector", vna)
        b.wait("vector", vma)
        def vn(e):
            inst = None
            for c in range(DCH):
                inst = e.tensor_tensor(WF[:, XLN + c * TS:XLN + (c + 1) * TS],
                                       WF[:, XLN + c * TS:XLN + (c + 1) * TS], PS[7][:, 0:TS],
                                       op=ALU.mult)
            return inst
        vm = b.opm("vector", vn)
        b.wait("scalar", vm)
        def sh(e):
            inst = None
            for c in range(DCH):
                inst = e.copy(WB[:, XH + c * TS:XH + (c + 1) * TS],
                              WF[:, XLN + c * TS:XLN + (c + 1) * TS])
            return inst
        sm = b.opm("scalar", sh)
        b.wait("vector", sm)
        def vl(e):
            inst = None
            for c in range(DCH):
                inst = e.tensor_tensor(WB[:, XL + c * TS:XL + (c + 1) * TS],
                                       WF[:, XLN + c * TS:XLN + (c + 1) * TS],
                                       WB[:, XH + c * TS:XH + (c + 1) * TS], op=ALU.subtract)
            return inst
        vsplit = b.opm("vector", vl)

        projs = [("W_igT", "ig"), ("W_rgT", "rg"), ("W_inT", "in")]
        evms = []
        for pi, (wname, tag) in enumerate(projs):
            for k in range(DCH):
                par = k % 2
                whi = WB[:, WHI + par * 2048:WHI + par * 2048 + 1024]
                wlo = WB[:, WHI + par * 2048 + 1024:WHI + par * 2048 + 2048]
                if lastmm[par] is not None:
                    b.wait("sync", lastmm[par])
                if si == 0:
                    wsp = WF[:, WSP + par * 1024:WSP + (par + 1) * 1024]
                    tok = b.dma("sync", wsp, g[wname + "_h"][k * 128:(k + 1) * 128, :],
                                sem="qW%d" % par)
                    b.wait("scalar", tok)
                    if spl0[par] is not None:
                        b.wait("scalar", spl0[par])
                    def shw(e, wsp=wsp, whi=whi):
                        return e.copy(whi, wsp)
                    smw = b.opm("scalar", shw)
                    b.wait("vector", smw)
                    def vlw(e, wsp=wsp, whi=whi, wlo=wlo):
                        return e.tensor_tensor(wlo, wsp, whi, op=ALU.subtract)
                    vmw = b.opm("vector", vlw)
                    b.wait("gpsimd", vmw)
                    spl0[par] = b.dma("gpsimd",
                                      g["l0w_scr"][pi * D + k * 128:pi * D + (k + 1) * 128, :],
                                      WB[:, WHI + par * 2048:WHI + par * 2048 + 2048],
                                      sem="qX%d" % par)
                    b.wait("tensor", vmw, vsplit)
                else:
                    if spl0[par] is not None:
                        b.wait("sync", spl0[par])
                        spl0[par] = None
                    tok = b.dma("sync", WB[:, WHI + par * 2048:WHI + par * 2048 + 2048],
                                g["l0w_scr"][pi * D + k * 128:pi * D + (k + 1) * 128, :],
                                sem="qW%d" % par)
                    b.wait("tensor", tok, vsplit)
                def mm(e, k=k, whi=whi, wlo=wlo):
                    xhs = WB[:, XH + k * TS:XH + (k + 1) * TS]
                    xls = WB[:, XL + k * TS:XL + (k + 1) * TS]
                    inst = None
                    for mi in range(8):
                        wc_hi = whi[:, mi * 128:(mi + 1) * 128]
                        wc_lo = wlo[:, mi * 128:(mi + 1) * 128]
                        e.matmul(PS[mi][:, 0:TS], wc_hi, xhs, start=(k == 0), stop=False)
                        e.matmul(PS[mi][:, 0:TS], wc_hi, xls, start=False, stop=False)
                        inst = e.matmul(PS[mi][:, 0:TS], wc_lo, xhs, start=False, stop=(k == DCH - 1))
                    return inst
                lastmm[par] = b.opm("tensor", mm)
            tmm = lastmm[(DCH - 1) % 2]
            b.wait("scalar", tmm)
            if tag == "ig":
                def ev(e):
                    inst = None
                    for mi in range(8):
                        inst = e.activation(WF[:, IBUF + mi * TS:IBUF + (mi + 1) * TS],
                                            PS[mi][:, 0:TS], AF.Sigmoid, bias=B_IG[:, mi:mi + 1])
                    return inst
                evm = b.opm("scalar", ev)
            elif tag == "rg":
                def ev(e):
                    inst = None
                    for mi in range(8):
                        inst = e.activation(WF[:, RBUF + mi * TS:RBUF + (mi + 1) * TS],
                                            PS[mi][:, 0:TS], AF.Sigmoid, bias=B_RG[:, mi:mi + 1])
                    return inst
                evm = b.opm("scalar", ev)
                b.wait("scalar", evm)
                def ev2(e):
                    inst = None
                    for mi in range(8):
                        inst = e.activation(WF[:, RBUF + mi * TS:RBUF + (mi + 1) * TS],
                                            WF[:, RBUF + mi * TS:RBUF + (mi + 1) * TS],
                                            AF.Exp, scale=LOGA8[:, mi:mi + 1])
                    return inst
                evm = b.opm("scalar", ev2)
            else:
                def ev(e):
                    inst = None
                    for mi in range(8):
                        inst = e.copy(WF[:, XLN + mi * TS:XLN + (mi + 1) * TS], PS[mi][:, 0:TS])
                    return inst
                evm = b.opm("scalar", ev)
            b.wait("tensor", evm)
            evms.append(evm)

        b.wait("vector", *evms)
        def vg(e):
            inst = None
            for c in range(DCH):
                i_ = WF[:, IBUF + c * TS:IBUF + (c + 1) * TS]
                xp = WF[:, XLN + c * TS:XLN + (c + 1) * TS]
                inst = e.tensor_tensor(xp, i_, xp, op=ALU.mult)
            return inst
        vgm = b.opm("vector", vg)
        b.wait("scalar", vgm, b.last["scalar"])
        def sga(e):
            inst = None
            for c in range(DCH):
                inst = e.activation(WF[:, IBUF + c * TS:IBUF + (c + 1) * TS],
                                    WF[:, RBUF + c * TS:RBUF + (c + 1) * TS], AF.Square)
            return inst
        sgam = b.opm("scalar", sga)
        b.wait("vector", sgam)
        def vgb(e):
            inst = None
            for c in range(DCH):
                inst = e.tensor_scalar(WF[:, IBUF + c * TS:IBUF + (c + 1) * TS],
                                       WF[:, IBUF + c * TS:IBUF + (c + 1) * TS],
                                       -1.0, 1.0 + 1e-6, op0=ALU.mult, op1=ALU.add)
            return inst
        vgbm = b.opm("vector", vgb)
        b.wait("scalar", vgbm)
        def sg(e):
            inst = None
            for c in range(DCH):
                inst = e.activation(WF[:, IBUF + c * TS:IBUF + (c + 1) * TS],
                                    WF[:, IBUF + c * TS:IBUF + (c + 1) * TS], AF.Sqrt)
            return inst
        sgm = b.opm("scalar", sg)
        b.wait("vector", sgm)
        def vg2(e):
            inst = None
            for c in range(DCH):
                inst = e.tensor_tensor(WF[:, XLN + c * TS:XLN + (c + 1) * TS],
                                       WF[:, XLN + c * TS:XLN + (c + 1) * TS],
                                       WF[:, IBUF + c * TS:IBUF + (c + 1) * TS], op=ALU.mult)
            return inst
        vg2m = b.opm("vector", vg2)
        b.wait("vector", vg2m)
        def vsc(e):
            inst = None
            for c in range(DCH):
                inst = e.tensor_tensor_scan(WF[:, HBUF + c * TS:HBUF + (c + 1) * TS],
                                            WF[:, RBUF + c * TS:RBUF + (c + 1) * TS],
                                            WF[:, XLN + c * TS:XLN + (c + 1) * TS],
                                            HST[:, c:c + 1], op0=ALU.mult, op1=ALU.add)
            return inst
        vscm = b.opm("vector", vsc)
        b.wait("vector", vscm)
        def vhst(e):
            inst = None
            for c in range(DCH):
                inst = e.tensor_copy(HST[:, c:c + 1],
                                     WF[:, HBUF + (c + 1) * TS - 1:HBUF + (c + 1) * TS])
            return inst
        vscan = b.opm("vector", vhst)

        b.wait("scalar", vscan)
        def shh(e):
            inst = None
            for c in range(DCH):
                inst = e.activation(WF[:, XLN + c * TS:XLN + (c + 1) * TS],
                                    WF[:, HBUF + c * TS:HBUF + (c + 1) * TS], AF.Square)
            return inst
        sqm = b.opm("scalar", shh)
        b.wait("tensor", sqm)
        def trms(e, ts=ts):
            inst = None
            for c in range(DCH):
                inst = e.matmul(PS[6][0:1, 0:TS], ONES, WF[:, XLN + c * TS:XLN + (c + 1) * TS],
                                start=(c == 0), stop=(c == DCH - 1))
            return inst
        trm = b.opm("tensor", trms)
        b.wait("scalar", trm)
        def srms(e, ts=ts, RMS=RMS):
            return e.activation(RMS, PS[6][0:1, 0:TS], AF.Sqrt, scale=1.0 / D, bias=1e-6)
        srm = b.opm("scalar", srms)
        b.wait("vector", srm)
        def vrms(e, RMS=RMS):
            return e.reciprocal(RMS, RMS)
        vrm = b.opm("vector", vrms)
        bm2 = bcast_rows(b, g, [(RMS, TS)], [7], [vrm])
        b.wait("vector", bm2)
        def vhn(e):
            inst = None
            for c in range(DCH):
                inst = e.scalar_tensor_tensor(WF[:, XLN + c * TS:XLN + (c + 1) * TS],
                                              WF[:, HBUF + c * TS:HBUF + (c + 1) * TS],
                                              NORMW[:, c:c + 1], PS[7][:, 0:TS],
                                              op0=ALU.mult, op1=ALU.mult)
            return inst
        vhm = b.opm("vector", vhn)
        b.wait("scalar", vhm)
        def shn(e):
            inst = None
            for c in range(DCH):
                inst = e.copy(WB[:, HNH + c * TS:HNH + (c + 1) * TS],
                              WF[:, XLN + c * TS:XLN + (c + 1) * TS])
            return inst
        shm = b.opm("scalar", shn)
        b.wait("vector", shm)
        def vhl(e):
            inst = None
            for c in range(DCH):
                inst = e.tensor_tensor(WB[:, HNL + c * TS:HNL + (c + 1) * TS],
                                       WF[:, XLN + c * TS:XLN + (c + 1) * TS],
                                       WB[:, HNH + c * TS:HNH + (c + 1) * TS], op=ALU.subtract)
            return inst
        vhsp = b.opm("vector", vhl)

        for k in range(DCH):
            par = k % 2
            whi = WB[:, WHI + par * 2048:WHI + par * 2048 + 1024]
            wlo = WB[:, WHI + par * 2048 + 1024:WHI + par * 2048 + 2048]
            if lastmm[par] is not None:
                b.wait("sync", lastmm[par])
            if si == 0:
                wsp = WF[:, WSP + par * 1024:WSP + (par + 1) * 1024]
                tok = b.dma("sync", wsp, g["W_outT_h"][k * 128:(k + 1) * 128, :],
                            sem="qW%d" % par)
                b.wait("scalar", tok)
                if spl0[par] is not None:
                    b.wait("scalar", spl0[par])
                def shw(e, wsp=wsp, whi=whi):
                    return e.copy(whi, wsp)
                smw = b.opm("scalar", shw)
                b.wait("vector", smw)
                def vlw(e, wsp=wsp, whi=whi, wlo=wlo):
                    return e.tensor_tensor(wlo, wsp, whi, op=ALU.subtract)
                vmw = b.opm("vector", vlw)
                b.wait("gpsimd", vmw)
                spl0[par] = b.dma("gpsimd",
                                  g["l0w_scr"][3 * D + k * 128:3 * D + (k + 1) * 128, :],
                                  WB[:, WHI + par * 2048:WHI + par * 2048 + 2048],
                                  sem="qX%d" % par)
                b.wait("tensor", vmw, vhsp)
            else:
                if spl0[par] is not None:
                    b.wait("sync", spl0[par])
                    spl0[par] = None
                tok = b.dma("sync", WB[:, WHI + par * 2048:WHI + par * 2048 + 2048],
                            g["l0w_scr"][3 * D + k * 128:3 * D + (k + 1) * 128, :],
                            sem="qW%d" % par)
                b.wait("tensor", tok, vhsp)
            def mmo(e, k=k, whi=whi, wlo=wlo):
                hh = WB[:, HNH + k * TS:HNH + (k + 1) * TS]
                hl = WB[:, HNL + k * TS:HNL + (k + 1) * TS]
                inst = None
                for mi in range(8):
                    e.matmul(PS[mi][:, 0:TS], whi[:, mi * 128:(mi + 1) * 128], hh,
                             start=(k == 0), stop=False)
                    e.matmul(PS[mi][:, 0:TS], whi[:, mi * 128:(mi + 1) * 128], hl,
                             start=False, stop=False)
                    inst = e.matmul(PS[mi][:, 0:TS], wlo[:, mi * 128:(mi + 1) * 128], hh,
                                    start=False, stop=(k == DCH - 1))
                return inst
            lastmm[par] = b.opm("tensor", mmo)
        tmo = lastmm[(DCH - 1) % 2]
        b.wait("vector", tmo)
        def vres(e, ts=ts):
            inst = None
            for mi in range(8):
                inst = e.tensor_tensor(X[:, mi * T + ts:mi * T + ts + TS],
                                       X[:, mi * T + ts:mi * T + ts + TS], PS[mi][:, 0:TS],
                                       op=ALU.add)
            return inst
        vrm2 = b.opm("vector", vres)
        b.wait("tensor", vrm2)

    b.barrier()
    emit_l0ffn(b, g)
    b.barrier()
    emit_l1attn(b, g)
    b.barrier()
    emit_l1ffn(b, g)
    b.barrier()

    for c in range(DCH):
        st = b.dma("gpsimd", g["out_h"][c * 128:(c + 1) * 128, :], X[:, c * T:(c + 1) * T])
    b.wait("gpsimd", st)


def kernel(**inputs):
    def TT(a):
        return np.ascontiguousarray(np.asarray(a, dtype=np.float32).T)

    def pv(v):
        return np.ascontiguousarray(np.asarray(v, dtype=np.float32).reshape(DCH, 128).T)

    x = np.asarray(inputs["L0_attn_x"], dtype=np.float32)
    l0vecs = np.concatenate([pv(inputs["L0_attn_b_ig"]), pv(inputs["L0_attn_b_rg"]),
                             pv(inputs["L0_attn_lam"]), pv(inputs["L0_attn_norm_w"]),
                             pv(np.asarray(inputs["L0_attn_h0"]).reshape(-1))], axis=1)
    convw = np.asarray(inputs["L0_ffn_conv_w"], dtype=np.float32).reshape(FCH, 128, 1, 3)[:, :, 0, :]
    convw = np.ascontiguousarray(convw.transpose(1, 2, 0).reshape(128, 3 * FCH))
    convb = np.ascontiguousarray(
        np.asarray(inputs["L0_ffn_conv_b"], dtype=np.float32).reshape(FCH, 128).T)
    convw = np.concatenate([convw, convb], axis=1)

    l1vecs = np.concatenate([pv(inputs[k]) for k in
                             ["L1_ffn_d_bq", "L1_ffn_d_bk", "L1_ffn_d_bv", "L1_ffn_d_bo",
                              "L1_ffn_c_bq", "L1_ffn_c_bk", "L1_ffn_c_bv", "L1_ffn_c_bo"]]
                            + [pv(np.zeros(D))] * 8, axis=1)
    ident = np.eye(128, dtype=np.float32)
    onesc = np.ones((128, 2), dtype=np.float32)
    onesr = np.ones((1, 128), dtype=np.float32)
    triu = np.triu(np.ones((128, 128), dtype=np.float32))
    sq = TT(np.asarray(inputs["L1_ffn_slot_query"])[0])
    sq_fm = np.concatenate([sq[c * 128:(c + 1) * 128] for c in range(DCH)], axis=1)

    base = {
        "W_inT": TT(inputs["L0_attn_W_in"]), "W_igT": TT(inputs["L0_attn_W_ig"]),
        "W_rgT": TT(inputs["L0_attn_W_rg"]), "W_outT": TT(inputs["L0_attn_W_out"]),
        "w1T": TT(inputs["L0_ffn_w1"]), "w2T": TT(inputs["L0_ffn_w2"]),
        "Wq": np.ascontiguousarray(np.asarray(inputs["L1_attn_Wq"], np.float32)),
        "Wk": np.ascontiguousarray(np.asarray(inputs["L1_attn_Wk"], np.float32)),
        "Wv": np.ascontiguousarray(np.asarray(inputs["L1_attn_Wv"], np.float32)),
        "Wo": np.ascontiguousarray(np.asarray(inputs["L1_attn_Wo"], np.float32)),
        "dWq": np.ascontiguousarray(np.asarray(inputs["L1_ffn_d_Wq"], np.float32)),
        "dWk": np.ascontiguousarray(np.asarray(inputs["L1_ffn_d_Wk"], np.float32)),
        "dWv": np.ascontiguousarray(np.asarray(inputs["L1_ffn_d_Wv"], np.float32)),
        "dWo": np.ascontiguousarray(np.asarray(inputs["L1_ffn_d_Wo"], np.float32)),
        "cWq": np.ascontiguousarray(np.asarray(inputs["L1_ffn_c_Wq"], np.float32)),
        "cWk": np.ascontiguousarray(np.asarray(inputs["L1_ffn_c_Wk"], np.float32)),
        "cWv": np.ascontiguousarray(np.asarray(inputs["L1_ffn_c_Wv"], np.float32)),
        "cWo": np.ascontiguousarray(np.asarray(inputs["L1_ffn_c_Wo"], np.float32)),
        "conv_w": convw, "l0vecs": l0vecs, "l1vecs": l1vecs, "slotq": sq_fm,
        "ident": ident, "onesc": onesc, "onesr": onesr, "triu": triu,
        "constp": np.ascontiguousarray(
            np.broadcast_to(np.asarray(CONST_VALS, dtype=np.float32), (128, len(CONST_VALS)))),
        "dbv_row": np.ascontiguousarray(np.asarray(inputs["L1_ffn_d_bv"], np.float32).reshape(1, D)),
        "cbv_row": np.ascontiguousarray(np.asarray(inputs["L1_ffn_c_bv"], np.float32).reshape(1, D)),
    }
    eW1 = np.asarray(inputs["L1_ffn_e_W1"], np.float32)
    eWg = np.asarray(inputs["L1_ffn_e_Wg"], np.float32)
    eW2 = np.asarray(inputs["L1_ffn_e_W2"], np.float32)
    e_ln_g = np.asarray(inputs["L1_ffn_e_ln_g"], np.float32)
    e_ln_b = np.asarray(inputs["L1_ffn_e_ln_b"], np.float32)
    e_b1 = np.asarray(inputs["L1_ffn_e_b1"], np.float32)
    e_bg = np.asarray(inputs["L1_ffn_e_bg"], np.float32)
    e_b2 = np.asarray(inputs["L1_ffn_e_b2"], np.float32)

    in_maps = []
    for c in range(NCORE):
        m = dict(base)
        m["x"] = TT(x[c])
        ee = [2 * c, 2 * c + 1]
        p1t = np.zeros((128, 16), dtype=np.float32)
        p2t = np.zeros((128, 16), dtype=np.float32)
        for j in range(8):
            for e2 in range(2):
                p1t[16 * j + 2 * c + e2, 2 * j + e2] = 1.0
        for s in range(16):
            p2t[16 * (s // 2) + 8 * (s % 2) + c, s] = 1.0
        m["p1t"] = p1t
        m["p2t"] = p2t
        m["eW1"] = np.ascontiguousarray(eW1[ee].reshape(2 * D, FF))
        m["eWg"] = np.ascontiguousarray(eWg[ee].reshape(2 * D, FF))
        m["eW2"] = np.ascontiguousarray(eW2[ee].reshape(2 * FF, D))
        def pf(vv):
            return np.ascontiguousarray(vv.reshape(FCH, 128).T)
        m["eb"] = np.concatenate(
            [pv(e_ln_g[e]) for e in ee] + [pv(e_ln_b[e]) for e in ee]
            + [pf(e_b1[e]) for e in ee] + [pf(e_bg[e]) for e in ee]
            + [pv(e_b2[e]) for e in ee], axis=1)
        m["ebrow"] = np.ascontiguousarray(np.concatenate(
            [np.concatenate([e_b1[e].ravel(), e_bg[e].ravel(), e_b2[e].ravel()])
             for e in ee]).reshape(1, -1).astype(np.float32))
        in_maps.append(m)

    nc = build_program()
    res = run_bass_kernel_spmd(nc, in_maps, core_ids=list(range(NCORE)))
    out = np.stack([np.asarray(r["out"], dtype=np.float32)
                    .reshape(DCH, 128, T).reshape(D, T).T for r in res.results])
    return out


def emit_l0ffn(b, g):
    nc = b.nc
    X, WF, WB, CN, PS = g["X"], g["WF"], g["WB"], g["CN"], g["PS"]
    IDENT = CN[:, 0:128]
    ONESR = g["onesr_sb"]
    CONVW = CN[:, 336:336 + 4 * FCH]
    SC = CN[:, 314:330]
    acc, tmp, swcol = SC[:, 0:1], SC[:, 1:2], SC[:, 2:3]
    COLSB = CN[:, 332:333]
    mw1 = CN[0:1, 333 * 0 + 330:331]
    mw2 = CN[0:1, 331:332]

    b.phase("l0ffn_wq")
    lastdma = [None, None]
    mw1col, mw2col = SC[:, 3:4], SC[:, 4:5]
    for wname, scr, nch, chw, mwap, mwcol in (("w1T", "m1q_h", 8, FF, mw1, mw1col),
                                              ("w2T", "m2q_h", 32, D, mw2, mw2col)):
        vm = None
        lastrd = [None, None]
        b.wait("sync", ("eV", b.cnt["eV"]), ("eS", b.cnt["eS"]))
        for k in range(nch):
            par = k % 2
            wwin = WF[:, par * chw:(par + 1) * chw]
            if lastrd[par] is not None:
                b.wait("sync", lastrd[par])
            tok = b.dma("sync", wwin, g[wname + "_h"][k * 128:(k + 1) * 128, :], sem="qW%d" % par)
            b.wait("vector", tok)
            if vm is not None:
                b.wait("vector", vm)
            if k == 0:
                def v0(e, wwin=wwin):
                    return e.reduce_sum(acc, wwin, axis=AX.X, apply_absolute_value=True)
                vm = b.opm("vector", v0)
            else:
                def v1(e, wwin=wwin):
                    return e.reduce_sum(tmp, wwin, axis=AX.X, apply_absolute_value=True)
                vt = b.opm("vector", v1)
                b.wait("vector", vt)
                def v2(e):
                    return e.tensor_tensor(acc, acc, tmp, op=ALU.add)
                vm = b.opm("vector", v2)
            lastrd[par] = vm
        b.wait("tensor", vm)
        def t0(e):
            return e.transpose(PS[6][0:1, 0:128], acc, IDENT)
        tm = b.opm("tensor", t0)
        b.wait("vector", tm)
        def v3(e):
            return e.reduce_sum(mwap, PS[6][0:1, 0:128], axis=AX.X)
        vm = b.opm("vector", v3)
        b.wait("vector", vm)
        def v4(e, nel=float(nch * 128 * chw)):
            return e.tensor_scalar(mwap, mwap, 1.0 / nel, 1e-5, op0=ALU.mult, op1=ALU.max)
        vm = b.opm("vector", v4)
        b.wait("vector", vm)
        def v5(e):
            return e.reciprocal(tmp[0:1, 0:1], mwap)
        vm = b.opm("vector", v5)
        b.wait("tensor", vm)
        def t1(e):
            e.matmul(PS[6][:, 0:1], ONESR, tmp[0:1, 0:1], start=True, stop=True)
            return e.matmul(PS[7][:, 0:1], ONESR, mwap, start=True, stop=True)
        tm = b.opm("tensor", t1)
        b.wait("scalar", tm)
        def s0c(e, mwcol=mwcol):
            e.copy(swcol, PS[6][:, 0:1])
            return e.copy(mwcol, PS[7][:, 0:1])
        sm = b.opm("scalar", s0c)
        b.wait("sync", sm)
        lastsq2 = [None, None]
        lastvmq = None
        for k in range(nch):
            par = k % 2
            wwin = WF[:, par * chw:(par + 1) * chw]
            c1w = WF[:, 2 * chw:3 * chw]
            c2w = WF[:, 3 * chw:4 * chw]
            mwin = WB[:, par * chw:(par + 1) * chw]
            if lastsq2[par] is not None:
                b.wait("sync", lastsq2[par])
            tok = b.dma("sync", wwin, g[wname + "_h"][k * 128:(k + 1) * 128, :], sem="qW%d" % par)
            b.wait("vector", tok, sm)
            def vq(e, wwin=wwin):
                return e.tensor_scalar(wwin, wwin, swcol, None, op0=ALU.mult)
            vy = b.opm("vector", vq)
            b.wait("scalar", vy)
            if lastvmq is not None:
                b.wait("scalar", lastvmq)
            def sq1(e, wwin=wwin, c1w=c1w):
                return e.activation(c1w, wwin, AF.Sigmoid, scale=LBIG, bias=-LBIG * 0.5)
            m1 = b.opm("scalar", sq1)
            b.wait("scalar", m1)
            def sq2(e, wwin=wwin, c2w=c2w):
                return e.activation(c2w, wwin, AF.Sigmoid, scale=-LBIG, bias=-LBIG * 0.5)
            m2 = b.opm("scalar", sq2)
            lastsq2[par] = m2
            b.wait("vector", m2)
            if lastdma[par] is not None:
                b.wait("vector", lastdma[par])
            def vmq_(e, c1w=c1w, c2w=c2w, mwin=mwin):
                return e.tensor_tensor(mwin, c1w, c2w, op=ALU.subtract)
            vmq = b.opm("vector", vmq_)
            lastvmq = vmq
            b.wait("gpsimd", vmq)
            lastdma[par] = b.dma("gpsimd", g[scr][k * 128:(k + 1) * 128, :], mwin, sem="qX%d" % par)
        b.wait("gpsimd", lastdma[0], lastdma[1])
    b.barrier()

    b.phase("l0ffn_main")
    TQ = 256
    W_ = 258
    XF, Y1 = 0, 2064
    AB, S1B, R1B = 4128, 4448, 4768
    HB, CVB, SH, RWS = 5088, 7680, 10272, 18528
    HSTR = 324
    MU2 = WF[0:1, RWS:RWS + W_]
    INV2 = WF[0:1, RWS + 320:RWS + 320 + W_]
    M2A = WF[0:1, RWS + 640:RWS + 640 + W_]
    R1R = WF[0:1, RWS + 960:RWS + 960 + W_]
    M2B = WF[0:1, RWS + 1280:RWS + 1280 + W_]
    R2R = WF[0:1, RWS + 1600:RWS + 1600 + W_]
    N1S = [0, 2064]
    WWIN, N2 = 4128, 8192
    COLSB = CN[:, 332:333]

    def quant1_phase(qq):
        t0_ = qq * TQ
        ws = min(max(0, t0_ - 1), T - W_)
        N1 = N1S[qq % 2]
        sb = [("eV", b.cnt["eV"]), ("eS", b.cnt["eS"]), ("eT", b.cnt["eT"])]
        b.wait("sync", ("eT", b.cnt["eT"]))
        lnm = ln_stats(b, g, X, lambda c: c * T, ws, W_, MU2, INV2,
                       WF[:, XF:XF + W_ * DCH], 6, 1e-5, sb)
        bm = bcast_rows(b, g, [(MU2, W_), (INV2, W_)], [6, 7], [lnm])
        b.wait("vector", bm)
        def vna(e, ws=ws):
            inst = None
            for c in range(DCH):
                inst = e.tensor_tensor(WF[:, XF + c * W_:XF + (c + 1) * W_],
                                       X[:, c * T + ws:c * T + ws + W_], PS[6][:, 0:W_],
                                       op=ALU.subtract)
            return inst
        vma = b.opm("vector", vna)
        b.wait("vector", vma)
        def vnb(e):
            inst = None
            for c in range(DCH):
                inst = e.tensor_tensor(WF[:, XF + c * W_:XF + (c + 1) * W_],
                                       WF[:, XF + c * W_:XF + (c + 1) * W_], PS[7][:, 0:W_],
                                       op=ALU.mult)
            return inst
        vxln = b.opm("vector", vnb)

        ABX = AB
        ABN = CVB
        b.wait("vector", vxln)
        def va0(e):
            e.tensor_tensor(WF[:, ABX:ABX + W_], WF[:, XF:XF + W_],
                            WF[:, XF + W_:XF + 2 * W_], op=ALU.max)
            return e.tensor_tensor(WF[:, ABN:ABN + W_], WF[:, XF:XF + W_],
                                   WF[:, XF + W_:XF + 2 * W_], op=ALU.min)
        vam = b.opm("vector", va0)
        for c in range(2, DCH):
            b.wait("vector", vam)
            def vac(e, c=c):
                e.tensor_tensor(WF[:, ABX:ABX + W_], WF[:, ABX:ABX + W_],
                                WF[:, XF + c * W_:XF + (c + 1) * W_], op=ALU.max)
                return e.tensor_tensor(WF[:, ABN:ABN + W_], WF[:, ABN:ABN + W_],
                                       WF[:, XF + c * W_:XF + (c + 1) * W_], op=ALU.min)
            vam = b.opm("vector", vac)
        cm = cross_max_row(b, g, WF[:, ABX:ABX + W_], W_, M2A, COLSB, 5, [vam])
        cmn = cross_max_row(b, g, WF[:, ABN:ABN + W_], W_, R1R, COLSB, 5, [cm], op=ALU.min)
        b.wait("vector", cmn)
        def vm2aa(e):
            return e.tensor_scalar(R1R, R1R, -1.0, None, op0=ALU.mult)
        vneg = b.opm("vector", vm2aa)
        b.wait("vector", vneg)
        def vm2a(e):
            return e.tensor_tensor(M2A, M2A, R1R, op=ALU.max)
        vm2x = b.opm("vector", vm2a)
        b.wait("vector", vm2x)
        def vm2a2(e):
            return e.tensor_scalar(M2A, M2A, 1e-5, None, op0=ALU.max)
        vm2 = b.opm("vector", vm2a2)
        b.wait("vector", vm2)
        def vs1(e):
            return e.reciprocal(R1R, M2A)
        vr1 = b.opm("vector", vs1)
        b.wait("vector", vr1)
        def vs2(e):
            return e.tensor_scalar(R1R, R1R, 127.0, None, op0=ALU.mult)
        vs2m = b.opm("vector", vs2)
        bm2 = bcast_rows(b, g, [(R1R, W_)], [6], [vs2m])
        b.wait("scalar", bm2)
        def scb(e):
            return e.copy(WF[:, S1B:S1B + W_], PS[6][:, 0:W_])
        scbm = b.opm("scalar", scb)
        bm3 = bcast_rows(b, g, [(M2A, W_)], [7], [scbm])
        b.wait("vector", bm3)
        def vr1r(e):
            return e.tensor_scalar(WF[:, R1B:R1B + W_], PS[7][:, 0:W_],
                                   SC[:, 3:4], 1.0 / 127.0, op0=ALU.mult, op1=ALU.mult)
        scb2m = b.opm("vector", vr1r)

        rlast = scb2m
        for cc in range(0, DCH, 2):
            b.wait("vector", rlast)
            def vy(e, cc=cc):
                e.tensor_tensor(WF[:, Y1:Y1 + W_], WF[:, XF + cc * W_:XF + (cc + 1) * W_],
                                WF[:, S1B:S1B + W_], op=ALU.mult)
                return e.tensor_tensor(WF[:, Y1 + W_:Y1 + 2 * W_],
                                       WF[:, XF + (cc + 1) * W_:XF + (cc + 2) * W_],
                                       WF[:, S1B:S1B + W_], op=ALU.mult)
            vym = b.opm("vector", vy)
            rlast = round8(b, g, WF[:, Y1:Y1 + 2 * W_],
                           WF[:, Y1 + 2 * W_:Y1 + 4 * W_],
                           WF[:, Y1 + 4 * W_:Y1 + 6 * W_],
                           WF[:, Y1 + 6 * W_:Y1 + 8 * W_],
                           WB[:, N1 + cc * W_:N1 + (cc + 2) * W_], [vym])
        return rlast

    rlast = quant1_phase(0)
    lastw = [None, None]
    for q in range(8):
        t0_ = q * TQ
        ws = min(max(0, t0_ - 1), T - W_)
        co = t0_ - ws
        N1 = N1S[q % 2]
        b.wait("sync", ("eT", b.cnt["eT"]))

        vabs2 = None
        lastg = None
        for mg in range(4):
            for k in range(DCH):
                par = k % 2
                wwin = WB[:, WWIN + par * 1024:WWIN + (par + 1) * 1024]
                if lastw[par] is not None:
                    b.wait("sync", lastw[par])
                tok = b.dma("sync", wwin,
                            g["m1q_h"][k * 128:(k + 1) * 128, mg * 1024:(mg + 1) * 1024],
                            sem="qW%d" % par)
                b.wait("tensor", tok, rlast)
                def mm(e, k=k, wwin=wwin, N1=N1):
                    inst = None
                    for mi in range(8):
                        inst = e.matmul(PS[mi][:, 0:W_], wwin[:, mi * 128:(mi + 1) * 128],
                                        WB[:, N1 + k * W_:N1 + (k + 1) * W_],
                                        start=(k == 0), stop=(k == DCH - 1))
                    return inst
                lastw[par] = b.opm("tensor", mm)
            tmm = lastw[(DCH - 1) % 2]
            b.wait("vector", tmm)
            if lastg is not None:
                b.wait("vector", lastg)
            def vgz(e):
                inst = None
                for mi in range(8):
                    e.memset(WF[:, HB + mi * HSTR:HB + mi * HSTR + 1], 0.0)
                    inst = e.memset(WF[:, HB + mi * HSTR + 1 + W_:HB + mi * HSTR + 2 + W_], 0.0)
                return inst
            vz = b.opm("vector", vgz)
            b.wait("vector", vz)
            def vev(e):
                inst = None
                for mi in range(8):
                    inst = e.tensor_tensor(WF[:, HB + mi * HSTR + 1:HB + mi * HSTR + 1 + W_],
                                           PS[mi][:, 0:W_], WF[:, R1B:R1B + W_], op=ALU.mult)
                return inst
            vevm = b.opm("vector", vev)
            b.wait("tensor", vevm)
            b.wait("vector", vevm)
            def cv1(e, mg=mg, co=co):
                inst = None
                for mi in range(8):
                    cf = mg * 8 + mi
                    inst = e.tensor_scalar(WF[:, CVB + mi * HSTR:CVB + mi * HSTR + TQ],
                                           WF[:, HB + mi * HSTR + co:HB + mi * HSTR + co + TQ],
                                           CONVW[:, 0 * FCH + cf:0 * FCH + cf + 1], None,
                                           op0=ALU.mult)
                return inst
            c1m = b.opm("vector", cv1)
            b.wait("vector", c1m)
            def cv2(e, mg=mg, co=co):
                inst = None
                for mi in range(8):
                    cf = mg * 8 + mi
                    inst = e.scalar_tensor_tensor(WF[:, CVB + mi * HSTR:CVB + mi * HSTR + TQ],
                                                  WF[:, HB + mi * HSTR + co + 1:HB + mi * HSTR + co + 1 + TQ],
                                                  CONVW[:, 1 * FCH + cf:1 * FCH + cf + 1],
                                                  WF[:, CVB + mi * HSTR:CVB + mi * HSTR + TQ],
                                                  op0=ALU.mult, op1=ALU.add)
                return inst
            c2m = b.opm("vector", cv2)
            b.wait("vector", c2m)
            def cv3(e, mg=mg, co=co):
                inst = None
                for mi in range(8):
                    cf = mg * 8 + mi
                    inst = e.scalar_tensor_tensor(WF[:, CVB + mi * HSTR:CVB + mi * HSTR + TQ],
                                                  WF[:, HB + mi * HSTR + co + 2:HB + mi * HSTR + co + 2 + TQ],
                                                  CONVW[:, 2 * FCH + cf:2 * FCH + cf + 1],
                                                  WF[:, CVB + mi * HSTR:CVB + mi * HSTR + TQ],
                                                  op0=ALU.mult, op1=ALU.add)
                return inst
            c3m = b.opm("vector", cv3)
            lastg = c3m
            b.wait("scalar", c3m)
            def ssil(e, mg=mg):
                inst = None
                for mi in range(8):
                    cf = mg * 8 + mi
                    inst = e.activation(WF[:, SH + cf * W_ + 0:SH + cf * W_ + TQ],
                                        WF[:, CVB + mi * HSTR:CVB + mi * HSTR + TQ],
                                        AF.Silu, bias=CONVW[:, 3 * FCH + cf:3 * FCH + cf + 1])
                return inst
            ssm = b.opm("scalar", ssil)
            b.wait("vector", ssm)
            if vabs2 is not None:
                b.wait("vector", vabs2)
            AB2N = AB + 320
            if mg == 0:
                def vab(e):
                    e.tensor_tensor(WF[:, AB:AB + TQ], WF[:, SH:SH + TQ],
                                    WF[:, SH + W_:SH + W_ + TQ], op=ALU.max)
                    return e.tensor_tensor(WF[:, Y1:Y1 + TQ], WF[:, SH:SH + TQ],
                                           WF[:, SH + W_:SH + W_ + TQ], op=ALU.min)
                vabs2 = b.opm("vector", vab)
            for cf in range(mg * 8 + (2 if mg == 0 else 0), mg * 8 + 8):
                b.wait("vector", vabs2)
                def vab2(e, cf=cf):
                    e.tensor_tensor(WF[:, AB:AB + TQ], WF[:, AB:AB + TQ],
                                    WF[:, SH + cf * W_:SH + cf * W_ + TQ], op=ALU.max)
                    return e.tensor_tensor(WF[:, Y1:Y1 + TQ], WF[:, Y1:Y1 + TQ],
                                           WF[:, SH + cf * W_:SH + cf * W_ + TQ], op=ALU.min)
                vabs2 = b.opm("vector", vab2)

        cm2 = cross_max_row(b, g, WF[:, AB:AB + TQ], TQ, M2B, COLSB, 5, [vabs2])
        cm2n = cross_max_row(b, g, WF[:, Y1:Y1 + TQ], TQ, R2R[0:1, 0:TQ], COLSB, 5,
                             [cm2], op=ALU.min)
        b.wait("vector", cm2n)
        def vm2bn(e):
            return e.tensor_scalar(R2R[0:1, 0:TQ], R2R[0:1, 0:TQ], -1.0, None, op0=ALU.mult)
        vnb2 = b.opm("vector", vm2bn)
        b.wait("vector", vnb2)
        def vm2bx(e):
            return e.tensor_tensor(M2B[0:1, 0:TQ], M2B[0:1, 0:TQ], R2R[0:1, 0:TQ], op=ALU.max)
        vmbx = b.opm("vector", vm2bx)
        b.wait("vector", vmbx)
        def vm2b(e):
            return e.tensor_scalar(M2B[0:1, 0:TQ], M2B[0:1, 0:TQ], 1e-5, None, op0=ALU.max)
        vmb = b.opm("vector", vm2b)
        b.wait("vector", vmb)
        def vs1b(e):
            return e.reciprocal(R2R[0:1, 0:TQ], M2B[0:1, 0:TQ])
        vrb = b.opm("vector", vs1b)
        b.wait("vector", vrb)
        def vs2b(e):
            return e.tensor_scalar(R2R[0:1, 0:TQ], R2R[0:1, 0:TQ], 127.0, None, op0=ALU.mult)
        vsb = b.opm("vector", vs2b)
        bm4 = bcast_rows(b, g, [(R2R[0:1, 0:TQ], TQ)], [6], [vsb])
        b.wait("scalar", bm4)
        def scb3(e):
            return e.copy(WF[:, S1B:S1B + TQ], PS[6][:, 0:TQ])
        scb3m = b.opm("scalar", scb3)
        bm5 = bcast_rows(b, g, [(M2B[0:1, 0:TQ], TQ)], [7], [scb3m])
        b.wait("vector", bm5)
        def vr2r(e):
            return e.tensor_scalar(WF[:, R1B:R1B + TQ], PS[7][:, 0:TQ],
                                   SC[:, 4:5], 1.0 / 127.0, op0=ALU.mult, op1=ALU.mult)
        scb4m = b.opm("vector", vr2r)

        rlA = scb4m
        rlB = scb4m
        for cc in range(0, FCH, 4):
            b.wait("vector", rlA, rlB)
            def vy2(e, cc=cc):
                inst = None
                for j in range(4):
                    inst = e.tensor_tensor(WF[:, HB + j * TQ:HB + (j + 1) * TQ],
                                           WF[:, SH + (cc + j) * W_:SH + (cc + j) * W_ + TQ],
                                           WF[:, S1B:S1B + TQ], op=ALU.mult)
                return inst
            vy2m = b.opm("vector", vy2)
            outs = round8_multi(b, g, [
                (WF[:, HB:HB + 2 * TQ], WF[:, HB + 4 * TQ:HB + 6 * TQ],
                 WF[:, CVB:CVB + 2 * TQ], WF[:, CVB + 2 * TQ:CVB + 4 * TQ],
                 WB[:, N2 + cc * TQ:N2 + (cc + 2) * TQ]),
                (WF[:, HB + 2 * TQ:HB + 4 * TQ], WF[:, HB + 6 * TQ:HB + 8 * TQ],
                 WF[:, CVB + 4 * TQ:CVB + 6 * TQ], WF[:, CVB + 6 * TQ:CVB + 8 * TQ],
                 WB[:, N2 + (cc + 2) * TQ:N2 + (cc + 4) * TQ]),
            ], [vy2m])
            rlA, rlB = outs
            for j in range(4):
                k = cc + j
                par = k % 2
                wwin = WB[:, WWIN + par * 1024:WWIN + (par + 1) * 1024]
                if lastw[par] is not None:
                    b.wait("sync", lastw[par])
                tok = b.dma("sync", wwin, g["m2q_h"][k * 128:(k + 1) * 128, :],
                            sem="qW%d" % par)
                b.wait("tensor", tok, outs[j // 2])
                if k == 0:
                    b.wait("tensor", ("eS", b.cnt["eS"]))
                def mm2(e, k=k, wwin=wwin):
                    inst = None
                    for mi in range(8):
                        inst = e.matmul(PS[mi][:, 0:TQ], wwin[:, mi * 128:(mi + 1) * 128],
                                        WB[:, N2 + k * TQ:N2 + (k + 1) * TQ],
                                        start=(k == 0), stop=(k == FCH - 1))
                    return inst
                lastw[par] = b.opm("tensor", mm2)
        tm2 = lastw[(FCH - 1) % 2]
        b.wait("vector", tm2)
        def vev2a(e):
            inst = None
            for mi in range(8):
                inst = e.tensor_tensor(WF[:, HB + mi * TQ:HB + (mi + 1) * TQ],
                                       PS[mi][:, 0:TQ], WF[:, R1B:R1B + TQ], op=ALU.mult)
            return inst
        ve2a = b.opm("vector", vev2a)
        b.wait("tensor", ve2a)

        if q < 7:
            rlast = quant1_phase(q + 1)

        b.wait("vector", ve2a, ("eT", b.cnt["eT"]), ("eS", b.cnt["eS"]))
        def vev2b(e, t0_=t0_):
            inst = None
            for mi in range(8):
                inst = e.tensor_tensor(X[:, mi * T + t0_:mi * T + t0_ + TQ],
                                       X[:, mi * T + t0_:mi * T + t0_ + TQ],
                                       WF[:, HB + mi * TQ:HB + (mi + 1) * TQ], op=ALU.add)
            return inst
        ve2b = b.opm("vector", vev2b)
        b.wait("tensor", ve2b)




def emit_l1attn(b, g):
    b.phase("l1attn")
    """Causal linear attention (elu+1 feature map), chunked, exact."""
    nc = b.nc
    X, WF, WB, CN, PS = g["X"], g["WF"], g["WB"], g["CN"], g["PS"]
    IDENT = CN[:, 0:128]
    ONESR = g["onesr_sb"]
    TRIU = CN[:, 464:592]
    GR = 256
    NCH = GR // 128
    XLY = 0
    YSB = 14336
    QF = 2048
    KF = 4096
    VA = 6144
    KT = 8320
    SA = 9344
    RB = 9920
    AM = 10496
    ROWS = 10752
    WSP = 12288
    MU = WF[0:1, ROWS:ROWS + GR]
    INV = WF[0:1, ROWS + 512:ROWS + 512 + GR]
    DEN = WF[0:1, ROWS + 1024:ROWS + 1024 + 512]
    XH, XL, WHI, WLO = 0, 2048, 4096, 6144

    def vsz(e):
        return e.memset(WF[:, SA:SA + 520], 0.0)
    vS = b.opm("vector", vsz)
    b.wait("vector", vS)

    lastw = [None, None]
    lastsa = None
    spl = [None, None]
    for gr in range(T // GR):
        ts = gr * GR
        sb = [("eV", b.cnt["eV"]), ("eS", b.cnt["eS"]), ("eT", b.cnt["eT"])]
        b.sync_coarse()
        lnm = ln_stats(b, g, X, lambda c: c * T, ts, GR, MU, INV,
                       WF[:, XLY:XLY + GR * DCH], 6, 1e-5, sb)
        bm = bcast_rows(b, g, [(MU, GR), (INV, GR)], [6, 7], [lnm])
        b.wait("vector", bm)
        def vna(e, ts=ts):
            inst = None
            for c in range(DCH):
                inst = e.tensor_tensor(WF[:, XLY + c * GR:XLY + (c + 1) * GR],
                                       X[:, c * T + ts:c * T + ts + GR], PS[6][:, 0:GR],
                                       op=ALU.subtract)
            return inst
        vma = b.opm("vector", vna)
        b.wait("vector", vma)
        def vnb(e):
            inst = None
            for c in range(DCH):
                inst = e.tensor_tensor(WF[:, XLY + c * GR:XLY + (c + 1) * GR],
                                       WF[:, XLY + c * GR:XLY + (c + 1) * GR], PS[7][:, 0:GR],
                                       op=ALU.mult)
            return inst
        vxln = b.opm("vector", vnb)
        b.wait("scalar", vxln)
        def shx(e):
            inst = None
            for c in range(DCH):
                inst = e.copy(WB[:, XH + c * GR:XH + (c + 1) * GR],
                              WF[:, XLY + c * GR:XLY + (c + 1) * GR])
            return inst
        sxh = b.opm("scalar", shx)
        b.wait("vector", sxh)
        def vlx(e):
            inst = None
            for c in range(DCH):
                inst = e.tensor_tensor(WB[:, XL + c * GR:XL + (c + 1) * GR],
                                       WF[:, XLY + c * GR:XLY + (c + 1) * GR],
                                       WB[:, XH + c * GR:XH + (c + 1) * GR], op=ALU.subtract)
            return inst
        vspl = b.opm("vector", vlx)

        for mwi, (wname, dst) in enumerate((("Wq_h", QF), ("Wk_h", KF))):
            for k in range(DCH):
                par = k % 2
                whi = WB[:, 4096 + par * 2048:4096 + par * 2048 + 1024]
                wlo = WB[:, 4096 + par * 2048 + 1024:4096 + par * 2048 + 2048]
                if lastw[par] is not None:
                    b.wait("sync", lastw[par])
                if gr == 0:
                    wsp = WF[:, WSP + par * 1024:WSP + (par + 1) * 1024]
                    tok = b.dma("sync", wsp, g[wname][k * 128:(k + 1) * 128, :],
                                sem="qW%d" % par)
                    b.wait("scalar", tok)
                    if spl[par] is not None:
                        b.wait("scalar", spl[par])
                    def shw(e, wsp=wsp, whi=whi):
                        return e.copy(whi, wsp)
                    smw = b.opm("scalar", shw)
                    b.wait("vector", smw)
                    def vlw(e, wsp=wsp, whi=whi, wlo=wlo):
                        return e.tensor_tensor(wlo, wsp, whi, op=ALU.subtract)
                    vmw = b.opm("vector", vlw)
                    b.wait("gpsimd", vmw)
                    spl[par] = b.dma("gpsimd",
                                     g["wspl_scr"][mwi * D + k * 128:mwi * D + (k + 1) * 128, :],
                                     WB[:, 4096 + par * 2048:4096 + par * 2048 + 2048],
                                     sem="qX%d" % par)
                    b.wait("tensor", vmw, vspl)
                else:
                    if spl[par] is not None:
                        b.wait("sync", spl[par])
                        spl[par] = None
                    tok = b.dma("sync", WB[:, 4096 + par * 2048:4096 + par * 2048 + 2048],
                                g["wspl_scr"][mwi * D + k * 128:mwi * D + (k + 1) * 128, :],
                                sem="qW%d" % par)
                    b.wait("tensor", tok, vspl)
                def mm(e, k=k, whi=whi, wlo=wlo):
                    xhs = WB[:, XH + k * GR:XH + (k + 1) * GR]
                    xls = WB[:, XL + k * GR:XL + (k + 1) * GR]
                    inst = None
                    for mi in range(8):
                        wch = whi[:, mi * 128:(mi + 1) * 128]
                        wcl = wlo[:, mi * 128:(mi + 1) * 128]
                        e.matmul(PS[mi][:, 0:GR], wch, xhs, start=(k == 0), stop=False)
                        e.matmul(PS[mi][:, 0:GR], wch, xls, start=False, stop=False)
                        inst = e.matmul(PS[mi][:, 0:GR], wcl, xhs, start=False,
                                        stop=(k == DCH - 1))
                    return inst
                lastw[par] = b.opm("tensor", mm)
            tmm = lastw[(DCH - 1) % 2]
            FMT = XLY
            b.wait("vector", tmm)
            def vfa(e):
                inst = None
                for mi in range(8):
                    inst = e.tensor_scalar(WF[:, FMT + mi * GR:FMT + (mi + 1) * GR],
                                           PS[mi][:, 0:GR], 0.0, None, op0=ALU.min)
                return inst
            vfam = b.opm("vector", vfa)
            b.wait("scalar", vfam)
            def sfa(e):
                return e.activation(WF[:, FMT:FMT + 8 * GR], WF[:, FMT:FMT + 8 * GR], AF.Exp)
            sfam = b.opm("scalar", sfa)
            b.wait("vector", sfam)
            def vfb(e, dst=dst):
                inst = None
                for mi in range(8):
                    inst = e.scalar_tensor_tensor(WF[:, dst + mi * GR:dst + (mi + 1) * GR],
                                                  PS[mi][:, 0:GR], 0.0,
                                                  WF[:, FMT + mi * GR:FMT + (mi + 1) * GR],
                                                  op0=ALU.max, op1=ALU.add)
                return inst
            vlast = b.opm("vector", vfb)
            b.wait("tensor", vlast)

        def vones(e):
            va = WF[:, VA:VA + NCH * 1040].rearrange("p (c h s) -> p c h s", c=NCH, h=16)
            return e.memset(va[:, :, :, 64:65], 1.0)
        vom = b.opm("vector", vones)
        for k in range(DCH):
            par = k % 2
            whi = WB[:, 4096 + par * 2048:4096 + par * 2048 + 1024]
            wlo = WB[:, 4096 + par * 2048 + 1024:4096 + par * 2048 + 2048]
            if lastw[par] is not None:
                b.wait("sync", lastw[par])
            if gr == 0:
                wsp = WF[:, WSP + par * 1024:WSP + (par + 1) * 1024]
                tok = b.dma("sync", wsp, g["Wv_h"][k * 128:(k + 1) * 128, :], sem="qW%d" % par)
                b.wait("scalar", tok)
                if spl[par] is not None:
                    b.wait("scalar", spl[par])
                def shw(e, wsp=wsp, whi=whi):
                    return e.copy(whi, wsp)
                smw = b.opm("scalar", shw)
                b.wait("vector", smw)
                def vlw(e, wsp=wsp, whi=whi, wlo=wlo):
                    return e.tensor_tensor(wlo, wsp, whi, op=ALU.subtract)
                vmw = b.opm("vector", vlw)
                b.wait("gpsimd", vmw)
                spl[par] = b.dma("gpsimd",
                                 g["wspl_scr"][2 * D + k * 128:2 * D + (k + 1) * 128, :],
                                 WB[:, 4096 + par * 2048:4096 + par * 2048 + 2048],
                                 sem="qX%d" % par)
                b.wait("tensor", vmw, vspl)
            else:
                if spl[par] is not None:
                    b.wait("sync", spl[par])
                    spl[par] = None
                tok = b.dma("sync", WB[:, 4096 + par * 2048:4096 + par * 2048 + 2048],
                            g["wspl_scr"][2 * D + k * 128:2 * D + (k + 1) * 128, :],
                            sem="qW%d" % par)
                b.wait("tensor", tok, vspl)
            if k == 0:
                b.wait("tensor", vlast)
            def mmv(e, k=k, whi=whi, wlo=wlo):
                inst = None
                for c in range(NCH):
                    xhs = WB[:, XH + k * GR + c * 128:XH + k * GR + (c + 1) * 128]
                    xls = WB[:, XL + k * GR + c * 128:XL + k * GR + (c + 1) * 128]
                    for hf in range(2):
                        bank = 2 + 2 * c + hf
                        e.matmul(PS[bank][:, 0:512], xhs, whi[:, hf * 512:(hf + 1) * 512],
                                 start=(k == 0), stop=False)
                        e.matmul(PS[bank][:, 0:512], xls, whi[:, hf * 512:(hf + 1) * 512],
                                 start=False, stop=False)
                        inst = e.matmul(PS[bank][:, 0:512], xhs, wlo[:, hf * 512:(hf + 1) * 512],
                                        start=False, stop=(k == DCH - 1))
                return inst
            lastw[par] = b.opm("tensor", mmv)
        tmv = lastw[(DCH - 1) % 2]
        b.wait("vector", tmv, vom)
        def vev(e):
            inst = None
            for c in range(NCH):
                va = WF[:, VA + c * 1040:VA + (c + 1) * 1040].rearrange(
                    "p (h s) -> p h s", h=16)
                e.tensor_copy(va[:, 0:8, 0:64], PS[2 + 2 * c][:, 0:512].rearrange(
                    "p (h s) -> p h s", h=8))
                inst = e.tensor_copy(va[:, 8:16, 0:64], PS[3 + 2 * c][:, 0:512].rearrange(
                    "p (h s) -> p h s", h=8))
            return inst
        vevm = b.opm("vector", vev)
        b.wait("tensor", vevm)

        for c in range(NCH):
            lastevk = None
            for p in range(DCH):
                def tkt(e, p=p, c=c):
                    return e.transpose(PS[p % 2][:, 0:128],
                                       WF[:, KF + p * GR + c * 128:KF + p * GR + (c + 1) * 128],
                                       IDENT)
                tkm = b.opm("tensor", tkt)
                b.wait("scalar", tkm)
                if lastevk is not None:
                    b.wait("scalar", lastevk)
                def skt(e, p=p):
                    return e.copy(WF[:, KT + p * 128:KT + (p + 1) * 128], PS[p % 2][:, 0:128])
                lastevk = b.opm("scalar", skt)
                b.wait("tensor", lastevk)
            lastam = [None, None]
            ymarks = []
            for h in range(16):
                par = h % 2
                base = (h % 2) * 64
                qsl = WF[base:base + 64, QF + (h // 2) * GR + c * 128:QF + (h // 2) * GR + (c + 1) * 128]
                ksl = WF[base:base + 64, KF + (h // 2) * GR + c * 128:KF + (h // 2) * GR + (c + 1) * 128]
                if lastam[par] is not None:
                    b.wait("tensor", lastam[par])
                def mm1(e, ksl=ksl, qsl=qsl, par=par):
                    return e.matmul(PS[par][:, 0:128], ksl, qsl, start=True, stop=True)
                m1 = b.opm("tensor", mm1)
                b.wait("vector", m1)
                def vam(e, par=par):
                    return e.tensor_tensor(WF[:, AM + par * 128:AM + (par + 1) * 128],
                                           PS[par][:, 0:128], TRIU, op=ALU.mult)
                lastam[par] = b.opm("vector", vam)
                b.wait("tensor", lastam[par])
                yb = PS[2 + h // 4]
                ycol = (h % 4) * 128
                def mm23(e, h=h, c=c, par=par, yb=yb, ycol=ycol, base=base, qsl=qsl):
                    va = WF[:, VA + c * 1040 + h * 65:VA + c * 1040 + (h + 1) * 65]
                    e.matmul(yb[0:65, ycol:ycol + 128], va,
                             WF[:, AM + par * 128:AM + (par + 1) * 128], start=True, stop=False)
                    sa = WF[base:base + 64, SA + (h // 2) * 65:SA + (h // 2) * 65 + 65]
                    return e.matmul(yb[0:65, ycol:ycol + 128], sa, qsl, start=False, stop=True)
                ymarks.append(b.opm("tensor", mm23))
            b.wait("scalar", ymarks[-1])
            sd = None
            vscl = None
            for ybi in range(4):
                yb = PS[2 + ybi]
                def sde(e, yb=yb):
                    return e.copy(DEN, yb[64:65, 0:512])
                sd = b.opm("scalar", sde)
                b.wait("vector", sd)
                def vde(e):
                    return e.tensor_scalar(DEN, DEN, 1e-6, None, op0=ALU.add)
                vd1 = b.opm("vector", vde)
                b.wait("vector", vd1)
                def vdr(e):
                    return e.reciprocal(DEN, DEN)
                vd2 = b.opm("vector", vdr)
                for hh in range(4):
                    h = ybi * 4 + hh
                    b.wait("tensor", vd2)
                    if vscl is not None:
                        b.wait("tensor", vscl)
                    def tbc(e, hh=hh):
                        return e.matmul(PS[6][0:64, 0:128], ONESR[0:1, 0:64],
                                        DEN[0:1, hh * 128:(hh + 1) * 128], start=True, stop=True)
                    tb = b.opm("tensor", tbc)
                    b.wait("scalar", tb)
                    def src_(e):
                        return e.copy(WF[0:64, RB:RB + 128], PS[6][0:64, 0:128])
                    sr = b.opm("scalar", src_)
                    b.wait("vector", sr)
                    def vsc(e, yb=yb, hh=hh, h=h, c=c):
                        return e.tensor_tensor(
                            WF[0:64, YSB + h * 256 + c * 128:YSB + h * 256 + c * 128 + 128],
                            yb[0:64, hh * 128:hh * 128 + 128], WF[0:64, RB:RB + 128],
                            op=ALU.mult)
                    vscl = b.opm("vector", vsc)
            b.wait("tensor", vscl)
            smks = []
            for h in range(16):
                base = (h % 2) * 64
                pr = h // 2
                bank = 6 if pr < 7 else 7
                colo = (pr * 65) % 455 if pr < 7 else 0
                def mms(e, h=h, c=c, base=base, bank=bank, colo=colo):
                    ktm = WF[:, KT + 0:KT + 1024][:, h * 64:(h + 1) * 64]
                    va = WF[:, VA + c * 1040 + h * 65:VA + c * 1040 + (h + 1) * 65]
                    return e.matmul(PS[bank][base:base + 64, colo:colo + 65], ktm, va,
                                    start=True, stop=True, tile_position=(0, base))
                smks.append(b.opm("tensor", mms))
            b.wait("vector", smks[-1])
            def vsadd(e):
                e.tensor_tensor(WF[:, SA:SA + 455], WF[:, SA:SA + 455], PS[6][:, 0:455],
                                op=ALU.add)
                return e.tensor_tensor(WF[:, SA + 455:SA + 520], WF[:, SA + 455:SA + 520],
                                       PS[7][:, 0:65], op=ALU.add)
            vsa = b.opm("vector", vsadd)
            b.wait("tensor", vsa)

        for h in range(16):
            par = h % 2
            wsp = WF[0:64, WSP + par * 1024:WSP + (par + 1) * 1024]
            if lastw[par] is not None:
                b.wait("sync", lastw[par])
            tok = b.dma("sync", wsp, g["Wo_h"][h * 64:(h + 1) * 64, :], sem="qW%d" % par)
            b.wait("tensor", tok)
            def mmo(e, h=h, wsp=wsp):
                inst = None
                for mi in range(8):
                    inst = e.matmul(PS[mi][:, 0:GR], wsp[:, mi * 128:(mi + 1) * 128],
                                    WF[0:64, YSB + h * 256:YSB + (h + 1) * 256],
                                    start=(h == 0), stop=(h == 15))
                return inst
            lastw[par] = b.opm("tensor", mmo)
        tmo = lastw[15 % 2]
        b.wait("vector", tmo)
        def vres(e, ts=ts):
            inst = None
            for mi in range(8):
                inst = e.tensor_tensor(X[:, mi * T + ts:mi * T + ts + GR],
                                       X[:, mi * T + ts:mi * T + ts + GR],
                                       PS[mi][:, 0:GR], op=ALU.add)
            return inst
        vrm = b.opm("vector", vres)
        b.wait("tensor", vrm)


def emit_l1ffn(b, g):
    b.phase("ff_pre")
    """Slot cross-attn FFN: o1 = q + MHA(q,x,x); per-slot expert GLU (sharded via
    AllToAll); out = MHA(x, o2, o2). fp32-exact (3-pass bf16 for token-sized mats)."""
    nc = b.nc
    X, WF, WB, CN, PS = g["X"], g["WF"], g["WB"], g["CN"], g["PS"]
    IDENT = CN[:, 0:128]
    ONESR = g["onesr_sb"]
    GR = 256
    NCH = GR // 128
    MU4 = WF[0:1, 0:T]
    INV4 = WF[0:1, 2048:2048 + T]
    XLF = 4096
    KVG = 6144
    PTA = 8192
    SCR = 12288
    QHS, O1S, SQF, ATT = 14336, 14464, 14592, 14720
    AVP = 14848
    O1SM = 15872
    L1V, EBO = 16896, 17024
    DBVR, CBVR = 17200, 18224
    WSP = 19248
    P2 = 12288
    lv = WF[:, L1V:L1V + 128]
    D_BQ, D_BK, D_BV, D_BO = lv[:, 0:8], lv[:, 8:16], lv[:, 16:24], lv[:, 24:32]
    C_BQ, C_BK, C_BV, C_BO = lv[:, 32:40], lv[:, 40:48], lv[:, 48:56], lv[:, 56:64]
    eb = WF[:, EBO:EBO + 176]
    XH, XL, WHI, WLO = 0, 2048, 4096, 6144
    OH, OL = 8192, 8320

    b.wait("sync", ("eV", b.cnt["eV"]), ("eS", b.cnt["eS"]), ("eT", b.cnt["eT"]))
    b.dma("sync", WF[0:1, DBVR:DBVR + D], g["dbv_row_h"][:, :])
    b.dma("sync", WF[0:1, CBVR:CBVR + D], g["cbv_row_h"][:, :])
    b.dma("sync", WF[:, L1V:L1V + 128], g["l1vecs_h"][:, :])
    b.dma("sync", WF[:, EBO:EBO + 176], g["eb_h"][:, :])
    ldb = b.dma("sync", WF[:, SQF:SQF + 128], g["slotq_h"][:, :])
    b.wait("vector", ldb)
    def vbq(e):
        e.tensor_scalar(D_BQ, D_BQ, 0.125, None, op0=ALU.mult)
        return e.tensor_scalar(C_BQ, C_BQ, 0.125, None, op0=ALU.mult)
    vb8 = b.opm("vector", vbq)
    b.wait("vector", vb8)

    for g4 in range(T // 512):
        ts = g4 * 512
        sb = [("eV", b.cnt["eV"]), ("eS", b.cnt["eS"]), ("eT", b.cnt["eT"])]
        ln_stats(b, g, X, lambda c: c * T, ts, 512, MU4[0:1, ts:ts + 512],
                 INV4[0:1, ts:ts + 512], WF[:, XLF:XLF + 4096], 6, 1e-5, sb)

    lw = None
    for k in range(DCH):
        wsp = WF[:, WSP:WSP + 1024]
        if lw is not None:
            b.wait("sync", lw)
        tok = b.dma("sync", wsp, g["dWq_h"][k * 128:(k + 1) * 128, :], sem="qW0")
        b.wait("tensor", tok, ("eV", b.cnt["eV"]), ("eS", b.cnt["eS"]))
        def mmq(e, k=k, wsp=wsp):
            inst = None
            for mi in range(8):
                inst = e.matmul(PS[mi][:, 0:16], wsp[:, mi * 128:(mi + 1) * 128],
                                WF[:, SQF + k * 16:SQF + (k + 1) * 16],
                                start=(k == 0), stop=(k == DCH - 1))
            return inst
        lw = b.opm("tensor", mmq)
    b.wait("vector", lw)
    def sqe(e):
        inst = None
        for mi in range(8):
            inst = e.tensor_scalar(WF[:, QHS + mi * 16:QHS + (mi + 1) * 16], PS[mi][:, 0:16],
                                   0.125, D_BQ[:, mi:mi + 1], op0=ALU.mult, op1=ALU.add)
        return inst
    sqm = b.opm("vector", sqe)
    b.wait("tensor", sqm)

    b.phase("ff_b1")
    b.sync_coarse()
    lastw = [None, None]
    lastsp = None
    lastvm = None
    spd = [None, None]
    for gr in range(T // GR):
        ts = gr * GR
        bm = bcast_rows(b, g, [(MU4[0:1, ts:ts + GR], GR), (INV4[0:1, ts:ts + GR], GR)],
                        [6, 7], [("eV", b.cnt["eV"]), ("eS", b.cnt["eS"])])
        b.wait("vector", bm)
        def vna(e, ts=ts):
            inst = None
            for c in range(DCH):
                inst = e.tensor_tensor(WF[:, XLF + c * GR:XLF + (c + 1) * GR],
                                       X[:, c * T + ts:c * T + ts + GR], PS[6][:, 0:GR],
                                       op=ALU.subtract)
            return inst
        vma = b.opm("vector", vna)
        b.wait("vector", vma)
        def vnb(e):
            inst = None
            for c in range(DCH):
                inst = e.tensor_tensor(WF[:, XLF + c * GR:XLF + (c + 1) * GR],
                                       WF[:, XLF + c * GR:XLF + (c + 1) * GR], PS[7][:, 0:GR],
                                       op=ALU.mult)
            return inst
        vxln = b.opm("vector", vnb)
        b.wait("scalar", vxln)
        def shx(e):
            inst = None
            for c in range(DCH):
                inst = e.copy(WB[:, XH + c * GR:XH + (c + 1) * GR],
                              WF[:, XLF + c * GR:XLF + (c + 1) * GR])
            return inst
        sxh = b.opm("scalar", shx)
        b.wait("vector", sxh)
        def vlx(e):
            inst = None
            for c in range(DCH):
                inst = e.tensor_tensor(WB[:, XL + c * GR:XL + (c + 1) * GR],
                                       WF[:, XLF + c * GR:XLF + (c + 1) * GR],
                                       WB[:, XH + c * GR:XH + (c + 1) * GR], op=ALU.subtract)
            return inst
        vspl = b.opm("vector", vlx)

        for mw, (wname, tmaj) in enumerate((("dWk_h", False), ("dWv_h", True))):
            for k in range(DCH):
                par = k % 2
                whi = WB[:, WHI + par * 2048:WHI + par * 2048 + 1024]
                wlo = WB[:, WHI + par * 2048 + 1024:WHI + par * 2048 + 2048]
                if lastw[par] is not None:
                    b.wait("sync", lastw[par])
                if gr == 0:
                    wsp = WF[:, WSP:WSP + 1024]
                    if lastsp is not None:
                        b.wait("sync", lastsp)
                    if lastvm is not None:
                        b.wait("sync", lastvm)
                    tok = b.dma("sync", wsp, g[wname][k * 128:(k + 1) * 128, :], sem="qW0")
                    b.wait("scalar", tok)
                    if spd[par] is not None:
                        b.wait("scalar", spd[par])
                    def shw(e, wsp=wsp, whi=whi):
                        return e.copy(whi, wsp)
                    lastsp = b.opm("scalar", shw)
                    b.wait("vector", lastsp)
                    def vlw(e, wsp=wsp, whi=whi, wlo=wlo):
                        return e.tensor_tensor(wlo, wsp, whi, op=ALU.subtract)
                    vmw = b.opm("vector", vlw)
                    lastvm = vmw
                    b.wait("gpsimd", vmw)
                    spd[par] = b.dma("gpsimd",
                                     g["dw_scr"][mw * D + k * 128:mw * D + (k + 1) * 128, :],
                                     WB[:, WHI + par * 2048:WHI + par * 2048 + 2048],
                                     sem="qX%d" % par)
                    b.wait("tensor", vmw, vspl)
                else:
                    if spd[par] is not None:
                        b.wait("sync", spd[par])
                        spd[par] = None
                    if k == 0 and lastw[1] is not None:
                        b.wait("sync", lastw[1])
                    tok = b.dma("sync", WB[:, WHI + par * 2048:WHI + par * 2048 + 2048],
                                g["dw_scr"][mw * D + k * 128:mw * D + (k + 1) * 128, :],
                                sem="qW%d" % par)
                    b.wait("tensor", tok, vspl)
                if not tmaj:
                    def mmk(e, k=k, whi=whi, wlo=wlo):
                        xhs = WB[:, XH + k * GR:XH + (k + 1) * GR]
                        xls = WB[:, XL + k * GR:XL + (k + 1) * GR]
                        inst = None
                        for mi in range(8):
                            wch = whi[:, mi * 128:(mi + 1) * 128]
                            wcl = wlo[:, mi * 128:(mi + 1) * 128]
                            e.matmul(PS[mi][:, 0:GR], wch, xhs, start=(k == 0), stop=False)
                            e.matmul(PS[mi][:, 0:GR], wch, xls, start=False, stop=False)
                            inst = e.matmul(PS[mi][:, 0:GR], wcl, xhs, start=False,
                                            stop=(k == DCH - 1))
                        return inst
                    lastw[par] = b.opm("tensor", mmk)
                else:
                    def mmv(e, k=k, whi=whi, wlo=wlo):
                        inst = None
                        for c in range(NCH):
                            xhs = WB[:, XH + k * GR + c * 128:XH + k * GR + (c + 1) * 128]
                            xls = WB[:, XL + k * GR + c * 128:XL + k * GR + (c + 1) * 128]
                            for hf in range(2):
                                e.matmul(PS[2 * c + hf][:, 0:512], xhs,
                                         whi[:, hf * 512:(hf + 1) * 512],
                                         start=(k == 0), stop=False)
                                e.matmul(PS[2 * c + hf][:, 0:512], xls,
                                         whi[:, hf * 512:(hf + 1) * 512],
                                         start=False, stop=False)
                                inst = e.matmul(PS[2 * c + hf][:, 0:512], xhs,
                                                wlo[:, hf * 512:(hf + 1) * 512],
                                                start=False, stop=(k == DCH - 1))
                        return inst
                    lastw[par] = b.opm("tensor", mmv)
            tp = lastw[(DCH - 1) % 2]
            if not tmaj:
                b.wait("vector", tp)
                def skh(e):
                    inst = None
                    for mi in range(8):
                        inst = e.tensor_scalar(WF[:, KVG + mi * GR:KVG + (mi + 1) * GR],
                                               PS[mi][:, 0:GR], D_BK[:, mi:mi + 1], None,
                                               op0=ALU.add)
                    return inst
                skm = b.opm("vector", skh)
                b.wait("tensor", skm)
                b.wait("gpsimd", skm)
                if gr == 0:
                    b.wait("gpsimd", spd[0], spd[1])
                for mi in range(8):
                    spl = b.dma("gpsimd", g["kh_scr"][mi * 128:(mi + 1) * 128, ts:ts + GR],
                                WF[:, KVG + mi * GR:KVG + (mi + 1) * GR], sem="qX0")
            else:
                b.wait("tensor", tp)
                def mvb(e):
                    inst = None
                    for c in range(NCH):
                        for hf in range(2):
                            inst = e.matmul(PS[2 * c + hf][0:128, 0:512], ONESR[0:1, 0:128],
                                            WF[0:1, DBVR + hf * 512:DBVR + (hf + 1) * 512],
                                            start=False, stop=True, skip_group_check=True)
                    return inst
                tvb = b.opm("tensor", mvb)
                b.wait("vector", tvb, ("qX0", b.cnt["qX0"]))
                def vvh(e):
                    inst = None
                    for c in range(NCH):
                        for hf in range(2):
                            inst = e.tensor_copy(
                                WF[:, KVG + c * 1024 + hf * 512:KVG + c * 1024 + (hf + 1) * 512],
                                PS[2 * c + hf][:, 0:512])
                    return inst
                vvm = b.opm("vector", vvh)
                b.wait("tensor", vvm)
                b.wait("gpsimd", vvm)
                if gr == 0:
                    b.wait("gpsimd", spd[0], spd[1])
                for c in range(NCH):
                    spl = b.dma("gpsimd", g["vh_scr"][ts + c * 128:ts + (c + 1) * 128, :],
                                WF[:, KVG + c * 1024:KVG + (c + 1) * 1024], sem="qX0")
        b.wait("scalar", ("qX0", b.cnt["qX0"]))
        b.wait("vector", ("qX0", b.cnt["qX0"]))

    b.phase("ff_b2p1")
    lastpt = None
    lastkd = None
    b.wait("tensor", ("eV", b.cnt["eV"]), ("eS", b.cnt["eS"]))
    b.wait("sync", ("qX0", b.cnt["qX0"]))
    for h in range(16):
        base = (h % 2) * 64
        khs = WF[base:base + 64, SCR:SCR + T]
        if lastkd is not None:
            b.wait("sync", lastkd)
        tok = b.dma("sync", khs,
                    g["kh_scr"][(h // 2) * 128 + base:(h // 2) * 128 + base + 64, :],
                    sem="qW0")
        smk = None
        for j in range(4):
            b.wait("tensor", tok)
            if lastpt is not None:
                b.wait("tensor", lastpt)
            def mms(e, h=h, j=j, base=base, khs=khs):
                return e.matmul(PS[4 + j][0:16, 0:512],
                                WF[base:base + 64, QHS + (h // 2) * 16:QHS + (h // 2) * 16 + 16],
                                khs[:, j * 512:(j + 1) * 512], start=True, stop=True)
            smk = b.opm("tensor", mms)
        RS = AVP - 64
        b.wait("vector", smk)
        def vmx(e, RS=RS):
            inst = None
            for j in range(4):
                inst = e.reduce_max(WF[0:16, RS + j:RS + j + 1], PS[4 + j][0:16, 0:512],
                                    axis=AX.X)
            return inst
        vm1 = b.opm("vector", vmx)
        b.wait("vector", vm1)
        def vmx2(e, RS=RS):
            return e.tensor_reduce(WF[0:16, RS + 8:RS + 9], WF[0:16, RS:RS + 4],
                                   axis=AX.X, op=ALU.max)
        vm2 = b.opm("vector", vmx2)
        b.wait("vector", vm2)
        def vneg(e, RS=RS):
            return e.tensor_scalar(WF[0:16, RS + 8:RS + 9], WF[0:16, RS + 8:RS + 9],
                                   -1.0, None, op0=ALU.mult)
        vm3 = b.opm("vector", vneg)
        b.wait("scalar", vm3)
        def sexp(e, RS=RS):
            inst = None
            for j in range(4):
                inst = e.activation(WF[0:16, SCR + j * 512:SCR + (j + 1) * 512],
                                    PS[4 + j][0:16, 0:512], AF.Exp,
                                    bias=WF[0:16, RS + 8:RS + 9])
            return inst
        sxm = b.opm("scalar", sexp)
        b.wait("vector", sxm)
        def vsm(e, RS=RS):
            return e.reduce_sum(WF[0:16, RS + 9:RS + 10], WF[0:16, SCR:SCR + T], axis=AX.X)
        vs1 = b.opm("vector", vsm)
        b.wait("vector", vs1)
        def vrc(e, RS=RS):
            return e.reciprocal(WF[0:16, RS + 9:RS + 10], WF[0:16, RS + 9:RS + 10])
        vs2 = b.opm("vector", vrc)
        b.wait("vector", vs2)
        def vpn(e, RS=RS):
            return e.tensor_scalar(WF[0:16, SCR:SCR + T], WF[0:16, SCR:SCR + T],
                                   WF[0:16, RS + 9:RS + 10], None, op0=ALU.mult)
        vp = b.opm("vector", vpn)
        lastev = None
        for c in range(16):
            b.wait("tensor", vp)
            if lastev is not None:
                b.wait("tensor", lastev)
            def tpt(e, c=c):
                return e.transpose(PS[c % 2][0:128, 0:16],
                                   WF[0:16, SCR + c * 128:SCR + (c + 1) * 128],
                                   IDENT[0:16, 0:16])
            tm = b.opm("tensor", tpt)
            b.wait("scalar", tm)
            if lastev is not None:
                b.wait("scalar", lastev)
            def spt(e, c=c, h=h):
                return e.copy(WF[:, PTA + (c * 16 + h) * 16:PTA + (c * 16 + h) * 16 + 16],
                              PS[c % 2][0:128, 0:16])
            lastev = b.opm("scalar", spt)
        lastpt = lastev
        lastkd = lastev
        b.wait("tensor", lastpt)

    b.phase("ff_b2p2")
    b.sync_coarse()
    lastv = None
    for c in range(16):
        vhc = WF[:, SCR:SCR + 1024]
        if lastv is not None:
            b.wait("sync", lastv)
        tok = b.dma("sync", vhc, g["vh_scr"][c * 128:(c + 1) * 128, :], sem="qW0")
        b.wait("tensor", tok, lastpt)
        def mav(e, c=c, vhc=vhc):
            inst = None
            for h in range(16):
                bank = h // 8
                colo = (h % 8) * 64
                inst = e.matmul(PS[bank][0:16, colo:colo + 64],
                                WF[:, PTA + (c * 16 + h) * 16:PTA + (c * 16 + h) * 16 + 16],
                                vhc[:, (h * 64):(h + 1) * 64],
                                start=(c == 0 and h % 8 == 0), stop=(c == 15 and h % 8 == 7))
            return inst
        lastv = b.opm("tensor", mav)
    b.wait("vector", lastv)
    def vav(e):
        e.tensor_copy(WF[0:16, AVP:AVP + 512], PS[0][0:16, 0:512])
        return e.tensor_copy(WF[0:16, AVP + 512:AVP + 1024], PS[1][0:16, 0:512])
    vavm = b.opm("vector", vav)
    b.wait("tensor", vavm)
    lastev = None
    for c2 in range(8):
        def tat(e, c2=c2):
            return e.transpose(PS[2 + c2 % 2][0:128, 0:16],
                               WF[0:16, AVP + c2 * 128:AVP + (c2 + 1) * 128], IDENT[0:16, 0:16])
        tm = b.opm("tensor", tat)
        b.wait("scalar", tm)
        if lastev is not None:
            b.wait("scalar", lastev)
        def sat(e, c2=c2):
            return e.copy(WF[:, ATT + c2 * 16:ATT + (c2 + 1) * 16], PS[2 + c2 % 2][0:128, 0:16])
        lastev = b.opm("scalar", sat)
        b.wait("tensor", lastev)
    lw = None
    for k in range(DCH):
        wsp = WF[:, WSP:WSP + 1024]
        b.wait("sync", lw if lw is not None else lastev)
        tok = b.dma("sync", wsp, g["dWo_h"][k * 128:(k + 1) * 128, :], sem="qW0")
        b.wait("tensor", tok, lastev)
        def mmo(e, k=k, wsp=wsp):
            inst = None
            for mi in range(8):
                inst = e.matmul(PS[mi][:, 0:16], wsp[:, mi * 128:(mi + 1) * 128],
                                WF[:, ATT + k * 16:ATT + (k + 1) * 16],
                                start=(k == 0), stop=(k == DCH - 1))
            return inst
        lw = b.opm("tensor", mmo)
    b.wait("vector", lw)
    def vo1(e):
        inst = None
        for mi in range(8):
            inst = e.scalar_tensor_tensor(WF[:, O1S + mi * 16:O1S + (mi + 1) * 16],
                                          PS[mi][:, 0:16], D_BO[:, mi:mi + 1],
                                          WF[:, SQF + mi * 16:SQF + (mi + 1) * 16],
                                          op0=ALU.add, op1=ALU.add)
        return inst
    vo1m = b.opm("vector", vo1)
    b.wait("tensor", vo1m)
    lastev = None
    for c2 in range(8):
        def to1(e, c2=c2):
            return e.transpose(PS[c2 % 2][0:16, 0:128], WF[:, O1S + c2 * 16:O1S + (c2 + 1) * 16],
                               IDENT)
        tm = b.opm("tensor", to1)
        b.wait("scalar", tm)
        if lastev is not None:
            b.wait("scalar", lastev)
        def so1(e, c2=c2):
            return e.copy(WF[0:16, O1SM + c2 * 128:O1SM + (c2 + 1) * 128], PS[c2 % 2][0:16, 0:128])
        lastev = b.opm("scalar", so1)
        b.wait("tensor", lastev)

    b.phase("ff_exp")
    b.wait("gpsimd", lastev)
    st1 = b.dma("gpsimd", g["cc_in1"][:, :], WF[0:16, O1SM:O1SM + 1024], sem="qX0")
    b.wait("gpsimd", st1)
    def cc1(e):
        e.collective_compute("AllGather", ALU.bypass, replica_groups=[list(range(8))],
                             ins=[g["cc_in1"].opt()], outs=[g["cc_out1"].opt()]).then_inc(
            b.sems["cc"], 1)
    b.op("gpsimd", cc1)
    b.cnt["cc"] += 1
    b.wait("gpsimd", ("cc", b.cnt["cc"]))
    b.wait("sync", ("cc", b.cnt["cc"]))
    EFM = PTA
    HLN = PTA + 128
    UEX = PTA + 256
    GEX = PTA + 512
    UG = PTA + 768
    YEX = PTA + 1024
    SQB = PTA + 1152
    OSM2 = SCR
    GG = SCR + 1024
    P1 = SCR + 2048 - 16
    b.sync_coarse()
    gt = b.dma("sync", WF[:, GG:GG + 1024], g["cc_out1"][:, :], sem="qW0")
    ptok = b.dma("sync", WF[:, WSP + 1008:WSP + 1024], g["p1t_h"][:, :], sem="qW0")
    b.wait("tensor", ptok, ("eV", b.cnt["eV"]), ("eS", b.cnt["eS"]))
    def tsel(e):
        e.matmul(PS[4][0:16, 0:512], WF[:, WSP + 1008:WSP + 1024], WF[:, GG:GG + 512],
                 start=True, stop=True)
        return e.matmul(PS[5][0:16, 0:512], WF[:, WSP + 1008:WSP + 1024],
                        WF[:, GG + 512:GG + 1024], start=True, stop=True)
    tsm = b.opm("tensor", tsel)
    b.wait("vector", tsm)
    def vsel(e):
        e.tensor_copy(WF[0:16, OSM2:OSM2 + 512], PS[4][0:16, 0:512])
        return e.tensor_copy(WF[0:16, OSM2 + 512:OSM2 + 1024], PS[5][0:16, 0:512])
    vselm = b.opm("vector", vsel)
    gt = vselm
    b.wait("tensor", gt)
    lastev = None
    for c2 in range(8):
        def tef(e, c2=c2):
            return e.transpose(PS[c2 % 2][0:128, 0:16],
                               WF[0:16, OSM2 + c2 * 128:OSM2 + (c2 + 1) * 128], IDENT[0:16, 0:16])
        tm = b.opm("tensor", tef)
        b.wait("scalar", tm)
        if lastev is not None:
            b.wait("scalar", lastev)
        def sef(e, c2=c2):
            return e.copy(WF[:, EFM + c2 * 16:EFM + (c2 + 1) * 16], PS[c2 % 2][0:128, 0:16])
        lastev = b.opm("scalar", sef)
        b.wait("tensor", lastev)
    gt = lastev
    EFMT = WF[:, EFM:EFM + 128].rearrange("p (c j e) -> p c j e", j=8, e=2)
    b.wait("tensor", gt)
    b.wait("scalar", gt)
    b.wait("vector", gt)
    lastw2 = [None, None]
    lastbias = None
    lastu = None
    lastg = None
    lastyv = None
    for ee in range(2):
        def efsl(c2, ee=ee):
            return EFMT[:, c2, :, ee]
        b.wait("tensor", ("eV", b.cnt["eV"]), ("eS", b.cnt["eS"]))
        def tst(e, efsl=efsl):
            inst = None
            for c2 in range(DCH):
                inst = e.matmul(PS[6][0:1, 0:8], CN[:, 128:129], efsl(c2),
                                start=(c2 == 0), stop=(c2 == DCH - 1))
            return inst
        t1 = b.opm("tensor", tst)
        b.wait("scalar", t1)
        def sms(e):
            return e.activation(WF[0:1, SQB + 64:SQB + 72], PS[6][0:1, 0:8], AF.Copy,
                                scale=1.0 / D)
        sm1 = b.opm("scalar", sms)
        def ssq(e, efsl=efsl):
            inst = None
            for c2 in range(DCH):
                inst = e.activation(WF[:, SQB + c2 * 8:SQB + (c2 + 1) * 8], efsl(c2), AF.Square)
            return inst
        sq1 = b.opm("scalar", ssq)
        b.wait("tensor", sq1)
        def tsq(e):
            inst = None
            for c2 in range(DCH):
                inst = e.matmul(PS[6][0:1, 0:8], CN[:, 128:129],
                                WF[:, SQB + c2 * 8:SQB + (c2 + 1) * 8],
                                start=(c2 == 0), stop=(c2 == DCH - 1))
            return inst
        t2 = b.opm("tensor", tsq)
        b.wait("vector", t2, sm1)
        def vvra(e):
            return e.tensor_tensor(WF[0:1, SQB + 72:SQB + 80], WF[0:1, SQB + 64:SQB + 72],
                                   WF[0:1, SQB + 64:SQB + 72], op=ALU.mult)
        vv0 = b.opm("vector", vvra)
        b.wait("vector", vv0)
        def vvr(e):
            return e.scalar_tensor_tensor(WF[0:1, SQB + 72:SQB + 80], PS[6][0:1, 0:8], 1.0 / D,
                                          WF[0:1, SQB + 72:SQB + 80],
                                          op0=ALU.mult, op1=ALU.subtract)
        vv1 = b.opm("vector", vvr)
        b.wait("scalar", vv1)
        def svs(e):
            return e.activation(WF[0:1, SQB + 72:SQB + 80], WF[0:1, SQB + 72:SQB + 80],
                                AF.Sqrt, bias=1e-5)
        sv1 = b.opm("scalar", svs)
        b.wait("vector", sv1)
        def vvi(e):
            return e.reciprocal(WF[0:1, SQB + 72:SQB + 80], WF[0:1, SQB + 72:SQB + 80])
        vv2 = b.opm("vector", vvi)
        b.wait("tensor", vv2)
        def tbc(e):
            e.matmul(PS[6][0:128, 0:8], ONESR[0:1, 0:128], WF[0:1, SQB + 64:SQB + 72],
                     start=True, stop=True)
            return e.matmul(PS[7][0:128, 0:8], ONESR[0:1, 0:128], WF[0:1, SQB + 72:SQB + 80],
                            start=True, stop=True)
        tb = b.opm("tensor", tbc)
        b.wait("vector", tb)
        def vln(e, efsl=efsl):
            inst = None
            for c2 in range(DCH):
                inst = e.tensor_tensor(WF[:, HLN + c2 * 8:HLN + (c2 + 1) * 8],
                                       efsl(c2), PS[6][:, 0:8], op=ALU.subtract)
            return inst
        vl1 = b.opm("vector", vln)
        b.wait("vector", vl1)
        def vln2(e):
            inst = None
            for c2 in range(DCH):
                inst = e.tensor_tensor(WF[:, HLN + c2 * 8:HLN + (c2 + 1) * 8],
                                       WF[:, HLN + c2 * 8:HLN + (c2 + 1) * 8], PS[7][:, 0:8],
                                       op=ALU.mult)
            return inst
        vl2 = b.opm("vector", vln2)
        b.wait("vector", vl2)
        def vln3(e, ee=ee):
            inst = None
            for c2 in range(DCH):
                inst = e.tensor_scalar(WF[:, HLN + c2 * 8:HLN + (c2 + 1) * 8],
                                       WF[:, HLN + c2 * 8:HLN + (c2 + 1) * 8],
                                       eb[:, ee * 8 + c2:ee * 8 + c2 + 1],
                                       eb[:, 16 + ee * 8 + c2:16 + ee * 8 + c2 + 1],
                                       op0=ALU.mult, op1=ALU.add)
            return inst
        vl3 = b.opm("vector", vln3)
        b.sync_coarse()
        TMPU = AVP
        TMPG = SCR
        UGT = QHS
        BR1 = XLF
        WSPB = 6144
        YSM = SCR + 1024
        for q in range(4):
            for wi, wname in ((0, "eW1_h"), (1, "eWg_h")):
                for k in range(DCH):
                    par = k % 2
                    win = WF[:, WSP:WSP + 1024] if par == 0 else WF[:, WSPB:WSPB + 1024]
                    if lastw2[par] is not None:
                        b.wait("sync", lastw2[par])
                    tok = b.dma("sync", win,
                                g[wname][ee * D + k * 128:ee * D + (k + 1) * 128,
                                         q * 1024:(q + 1) * 1024], sem="qW%d" % par)
                    b.wait("tensor", tok, vl3)
                    if k == 0:
                        b.wait("tensor", lastu if wi == 0 else lastg)
                    def mmw(e, k=k, win=win, wi=wi):
                        e.matmul(PS[2 * wi][0:8, 0:512], WF[:, HLN + k * 8:HLN + (k + 1) * 8],
                                 win[:, 0:512], start=(k == 0), stop=(k == DCH - 1))
                        return e.matmul(PS[2 * wi + 1][0:8, 0:512],
                                        WF[:, HLN + k * 8:HLN + (k + 1) * 8],
                                        win[:, 512:1024], start=(k == 0), stop=(k == DCH - 1))
                    lastw2[par] = b.opm("tensor", mmw)
                bro = BR1 + wi * 1024
                if lastbias is not None:
                    b.wait("sync", lastbias)
                btok = b.dma("sync", WF[0:1, bro:bro + 1024],
                             g["ebrow_h"][0:1, ee * 9216 + wi * 4096 + q * 1024:
                                          ee * 9216 + wi * 4096 + (q + 1) * 1024], sem="qS")
                b.wait("tensor", btok, lastw2[0], lastw2[1])
                def mmb(e, wi=wi, bro=bro):
                    e.matmul(PS[2 * wi][0:8, 0:512], ONESR[0:1, 0:8], WF[0:1, bro:bro + 512],
                             start=False, stop=True, skip_group_check=True)
                    return e.matmul(PS[2 * wi + 1][0:8, 0:512], ONESR[0:1, 0:8],
                                    WF[0:1, bro + 512:bro + 1024],
                                    start=False, stop=True, skip_group_check=True)
                lastbias = b.opm("tensor", mmb)
                if wi == 0:
                    b1mm = lastbias
            b.wait("scalar", b1mm)
            def slu(e):
                e.activation(WF[0:8, TMPU:TMPU + 512], PS[0][0:8, 0:512], AF.Silu)
                return e.activation(WF[0:8, TMPU + 512:TMPU + 1024], PS[1][0:8, 0:512], AF.Silu)
            lastu = b.opm("scalar", slu)
            b.wait("scalar", lastbias)
            def slg(e):
                e.activation(WF[0:8, TMPG:TMPG + 512], PS[2][0:8, 0:512], AF.Silu)
                return e.activation(WF[0:8, TMPG + 512:TMPG + 1024], PS[3][0:8, 0:512], AF.Silu)
            lastg = b.opm("scalar", slg)
            b.wait("vector", lastu, lastg)
            def vmug(e):
                return e.tensor_tensor(WF[0:8, TMPU:TMPU + 1024], WF[0:8, TMPU:TMPU + 1024],
                                       WF[0:8, TMPG:TMPG + 1024], op=ALU.mult)
            ugm = b.opm("vector", vmug)
            lastc = None
            for i in range(8):
                b.wait("tensor", ugm)
                if lastc is not None:
                    b.wait("tensor", lastc)
                def tug(e, i=i):
                    return e.transpose(PS[4 + i % 2][0:128, 0:8],
                                       WF[0:8, TMPU + i * 128:TMPU + (i + 1) * 128],
                                       IDENT[0:8, 0:8])
                tm = b.opm("tensor", tug)
                b.wait("scalar", tm)
                if lastc is not None:
                    b.wait("scalar", lastc)
                def sug(e, i=i):
                    return e.copy(WF[:, UGT + i * 8:UGT + (i + 1) * 8], PS[4 + i % 2][0:128, 0:8])
                lastc = b.opm("scalar", sug)
            for k in range(DCH):
                par = k % 2
                win = WF[:, WSP:WSP + 1024] if par == 0 else WF[:, WSPB:WSPB + 1024]
                b.wait("sync", lastw2[par])
                tok = b.dma("sync", win,
                            g["eW2_h"][ee * FF + (q * 8 + k) * 128:
                                       ee * FF + (q * 8 + k + 1) * 128, :],
                            sem="qW%d" % par)
                b.wait("tensor", tok, lastc)
                if q == 0 and k == 0:
                    b.wait("tensor", lastyv)
                def mmy(e, k=k, q=q, win=win):
                    e.matmul(PS[6][0:8, 0:512], WF[:, UGT + k * 8:UGT + (k + 1) * 8],
                             win[:, 0:512], start=(q == 0 and k == 0),
                             stop=(q == 3 and k == DCH - 1))
                    return e.matmul(PS[7][0:8, 0:512], WF[:, UGT + k * 8:UGT + (k + 1) * 8],
                                    win[:, 512:1024], start=(q == 0 and k == 0),
                                    stop=(q == 3 and k == DCH - 1))
                lastw2[par] = b.opm("tensor", mmy)
        if lastbias is not None:
            b.wait("sync", lastbias)
        b2tok = b.dma("sync", WF[0:1, DBVR:DBVR + 1024],
                      g["ebrow_h"][0:1, ee * 9216 + 8192:ee * 9216 + 9216], sem="qS")
        b.wait("tensor", b2tok, lastw2[0], lastw2[1])
        def mmb2(e):
            e.matmul(PS[6][0:8, 0:512], ONESR[0:1, 0:8], WF[0:1, DBVR:DBVR + 512],
                     start=False, stop=True, skip_group_check=True)
            return e.matmul(PS[7][0:8, 0:512], ONESR[0:1, 0:8], WF[0:1, DBVR + 512:DBVR + 1024],
                            start=False, stop=True, skip_group_check=True)
        lastbias = b.opm("tensor", mmb2)
        b.wait("vector", lastbias)
        yo = YSM if ee == 0 else TMPG
        def vye(e, yo=yo):
            e.tensor_copy(WF[0:8, yo:yo + 512], PS[6][0:8, 0:512])
            return e.tensor_copy(WF[0:8, yo + 512:yo + 1024], PS[7][0:8, 0:512])
        lastyv = b.opm("vector", vye)
    lastev = lastyv
    b.wait("gpsimd", lastev)
    b.dma("gpsimd", g["cc_in2"][0:8, :], WF[0:8, YSM:YSM + 1024], sem="qX0")
    st = b.dma("gpsimd", g["cc_in2"][8:16, :], WF[0:8, TMPG:TMPG + 1024], sem="qX0")
    b.wait("gpsimd", st)
    def cc2(e):
        e.collective_compute("AllGather", ALU.bypass, replica_groups=[list(range(8))],
                             ins=[g["cc_in2"].opt()], outs=[g["cc_out2"].opt()]).then_inc(
            b.sems["cc"], 1)
    b.op("gpsimd", cc2)
    b.cnt["cc"] += 1
    b.wait("gpsimd", ("cc", b.cnt["cc"]))
    b.wait("sync", ("cc", b.cnt["cc"]))
    O2SM = AVP
    b.sync_coarse()
    gt2g = b.dma("sync", WF[:, GG:GG + 1024], g["cc_out2"][:, :], sem="qW0")
    ptok2 = b.dma("sync", WF[:, WSP + 1008:WSP + 1024], g["p2t_h"][:, :], sem="qW0")
    b.wait("tensor", ptok2, ("eV", b.cnt["eV"]), ("eS", b.cnt["eS"]))
    def tsel2(e):
        e.matmul(PS[4][0:16, 0:512], WF[:, WSP + 1008:WSP + 1024], WF[:, GG:GG + 512],
                 start=True, stop=True)
        return e.matmul(PS[5][0:16, 0:512], WF[:, WSP + 1008:WSP + 1024],
                        WF[:, GG + 512:GG + 1024], start=True, stop=True)
    tsm2 = b.opm("tensor", tsel2)
    b.wait("vector", tsm2)
    def vsel2(e):
        e.tensor_copy(WF[0:16, O2SM:O2SM + 512], PS[4][0:16, 0:512])
        return e.tensor_copy(WF[0:16, O2SM + 512:O2SM + 1024], PS[5][0:16, 0:512])
    gt2 = b.opm("vector", vsel2)
    b.wait("vector", gt2)
    def vo2(e):
        return e.tensor_tensor(WF[0:16, O2SM:O2SM + 1024], WF[0:16, O2SM:O2SM + 1024],
                               WF[0:16, O1SM:O1SM + 1024], op=ALU.add)
    vo2m = b.opm("vector", vo2)
    b.wait("tensor", vo2m)
    lastev = None
    for c2 in range(8):
        def to2(e, c2=c2):
            return e.transpose(PS[c2 % 2][0:128, 0:16],
                               WF[0:16, O2SM + c2 * 128:O2SM + (c2 + 1) * 128], IDENT[0:16, 0:16])
        tm = b.opm("tensor", to2)
        b.wait("scalar", tm)
        if lastev is not None:
            b.wait("scalar", lastev)
        def so2(e, c2=c2):
            return e.copy(WF[:, O1S + c2 * 16:O1S + (c2 + 1) * 16], PS[c2 % 2][0:128, 0:16])
        lastev = b.opm("scalar", so2)
        b.wait("tensor", lastev)
    b.wait("scalar", lastev)
    def sh2(e):
        return e.copy(WB[:, OH:OH + 128], WF[:, O1S:O1S + 128])
    sh2m = b.opm("scalar", sh2)
    b.wait("vector", sh2m)
    def vl2_(e):
        return e.tensor_tensor(WB[:, OL:OL + 128], WF[:, O1S:O1S + 128], WB[:, OH:OH + 128],
                               op=ALU.subtract)
    vl2m = b.opm("vector", vl2_)

    b.phase("ff_d")
    b.sync_coarse()
    lastvm = None
    KH2 = ATT
    VH2 = O2SM + 1024
    VH2 = O1SM
    lastw = [None, None]
    for k in range(DCH):
        par = k % 2
        wsp = WF[:, WSP:WSP + 1024]
        whi = WB[:, WHI + par * 1024:WHI + (par + 1) * 1024]
        wlo = WB[:, WLO + par * 1024:WLO + (par + 1) * 1024]
        if lastw[par] is not None:
            b.wait("sync", lastw[par])
        if lastsp is not None:
            b.wait("sync", lastsp)
        if lastvm is not None:
            b.wait("sync", lastvm)
        tok = b.dma("sync", wsp, g["cWk_h"][k * 128:(k + 1) * 128, :], sem="qW0")
        b.wait("scalar", tok, vl2m)
        def shw(e, wsp=wsp, whi=whi):
            return e.copy(whi, wsp)
        lastsp = b.opm("scalar", shw)
        b.wait("vector", lastsp)
        def vlw(e, wsp=wsp, whi=whi, wlo=wlo):
            return e.tensor_tensor(wlo, wsp, whi, op=ALU.subtract)
        vmw = b.opm("vector", vlw)
        lastvm = vmw
        b.wait("tensor", vmw)
        def mk2(e, k=k, whi=whi, wlo=wlo):
            inst = None
            for mi in range(8):
                oh = WB[:, OH + k * 16:OH + (k + 1) * 16]
                ol = WB[:, OL + k * 16:OL + (k + 1) * 16]
                e.matmul(PS[mi][:, 0:16], whi[:, mi * 128:(mi + 1) * 128], oh,
                         start=(k == 0), stop=False)
                e.matmul(PS[mi][:, 0:16], whi[:, mi * 128:(mi + 1) * 128], ol,
                         start=False, stop=False)
                inst = e.matmul(PS[mi][:, 0:16], wlo[:, mi * 128:(mi + 1) * 128], oh,
                                start=False, stop=(k == DCH - 1))
            return inst
        lastw[par] = b.opm("tensor", mk2)
    tk2 = lastw[(DCH - 1) % 2]
    b.wait("vector", tk2)
    def sk2(e):
        inst = None
        for mi in range(8):
            inst = e.tensor_scalar(WF[:, KH2 + mi * 16:KH2 + (mi + 1) * 16], PS[mi][:, 0:16],
                                   C_BK[:, mi:mi + 1], None, op0=ALU.add)
        return inst
    sk2m = b.opm("vector", sk2)
    b.wait("tensor", sk2m)
    for k in range(DCH):
        par = k % 2
        wsp = WF[:, WSP:WSP + 1024]
        whi = WB[:, WHI + par * 1024:WHI + (par + 1) * 1024]
        wlo = WB[:, WLO + par * 1024:WLO + (par + 1) * 1024]
        if lastw[par] is not None:
            b.wait("sync", lastw[par])
        b.wait("sync", lastsp)
        if lastvm is not None:
            b.wait("sync", lastvm)
        tok = b.dma("sync", wsp, g["cWv_h"][k * 128:(k + 1) * 128, :], sem="qW0")
        b.wait("scalar", tok)
        def shw(e, wsp=wsp, whi=whi):
            return e.copy(whi, wsp)
        lastsp = b.opm("scalar", shw)
        b.wait("vector", lastsp)
        def vlw(e, wsp=wsp, whi=whi, wlo=wlo):
            return e.tensor_tensor(wlo, wsp, whi, op=ALU.subtract)
        vmw = b.opm("vector", vlw)
        lastvm = vmw
        b.wait("tensor", vmw)
        def mv2(e, k=k, whi=whi, wlo=wlo):
            oh = WB[:, OH + k * 16:OH + (k + 1) * 16]
            ol = WB[:, OL + k * 16:OL + (k + 1) * 16]
            inst = None
            for hf in range(2):
                e.matmul(PS[4 + hf][0:16, 0:512], oh, whi[:, hf * 512:(hf + 1) * 512],
                         start=(k == 0), stop=False)
                e.matmul(PS[4 + hf][0:16, 0:512], ol, whi[:, hf * 512:(hf + 1) * 512],
                         start=False, stop=False)
                inst = e.matmul(PS[4 + hf][0:16, 0:512], oh, wlo[:, hf * 512:(hf + 1) * 512],
                                start=False, stop=(k == DCH - 1))
            return inst
        lastw[par] = b.opm("tensor", mv2)
    tv2 = lastw[(DCH - 1) % 2]
    b.wait("tensor", tv2)
    def mvb2(e):
        inst = None
        for hf in range(2):
            inst = e.matmul(PS[4 + hf][0:16, 0:512], ONESR[0:1, 0:16],
                            WF[0:1, CBVR + hf * 512:CBVR + (hf + 1) * 512],
                            start=False, stop=True, skip_group_check=True)
        return inst
    tvb2 = b.opm("tensor", mvb2)
    b.wait("vector", tvb2)
    def vv2(e):
        e.tensor_copy(WF[0:16, VH2:VH2 + 512], PS[4][0:16, 0:512])
        return e.tensor_copy(WF[0:16, VH2 + 512:VH2 + 1024], PS[5][0:16, 0:512])
    vv2m = b.opm("vector", vv2)
    b.wait("tensor", vv2m)
    K2B = PTA + 2048
    b.wait("vector", sk2m)
    def vkz(e):
        return e.memset(WF[:, K2B:K2B + 512], 0.0)
    vkz1 = b.opm("vector", vkz)
    b.wait("vector", vkz1)
    def vkb(e):
        K2V = WF[:, K2B:K2B + 512].rearrange("p (a b) -> p a b", b=64)
        KHV = WF[:, KH2:KH2 + 128].rearrange("p (a b) -> p a b", b=16)
        e.tensor_copy(K2V[0:64, :, 0:16], KHV[0:64, :, :])
        return e.tensor_copy(K2V[64:128, :, 32:48], KHV[64:128, :, :])
    vkbm = b.opm("vector", vkb)
    b.wait("tensor", vkbm, vv2m)
    def tv32(e):
        e.matmul(PS[2][32:48, 0:512], IDENT[0:16, 0:16], WF[0:16, VH2:VH2 + 512],
                 start=True, stop=True)
        return e.matmul(PS[3][32:48, 0:512], IDENT[0:16, 0:16],
                        WF[0:16, VH2 + 512:VH2 + 1024], start=True, stop=True)
    tv32m = b.opm("tensor", tv32)
    b.wait("vector", tv32m)
    def vv32(e):
        e.tensor_copy(WF[32:48, VH2:VH2 + 512], PS[2][32:48, 0:512])
        return e.tensor_copy(WF[32:48, VH2 + 512:VH2 + 1024], PS[3][32:48, 0:512])
    vv32m = b.opm("vector", vv32)
    b.wait("tensor", vv32m)

    b.phase("ff_e")
    QH2 = KVG
    AVF = PTA
    P2B, PTS, SMX, SSM = 12288, 12800, 13824, 13856
    lastw = [None, None]
    avprev = [None, None]
    lastavk = None
    lastpte = None
    sple = [None, None]
    for gr in range(T // GR):
        b.sync_coarse()
        ts = gr * GR
        bm = bcast_rows(b, g, [(MU4[0:1, ts:ts + GR], GR), (INV4[0:1, ts:ts + GR], GR)],
                        [6, 7], [("eV", b.cnt["eV"]), ("eS", b.cnt["eS"])])
        b.wait("vector", bm)
        def vna(e, ts=ts):
            inst = None
            for c in range(DCH):
                inst = e.tensor_tensor(WF[:, XLF + c * GR:XLF + (c + 1) * GR],
                                       X[:, c * T + ts:c * T + ts + GR], PS[6][:, 0:GR],
                                       op=ALU.subtract)
            return inst
        vma = b.opm("vector", vna)
        b.wait("vector", vma)
        def vnb(e):
            inst = None
            for c in range(DCH):
                inst = e.tensor_tensor(WF[:, XLF + c * GR:XLF + (c + 1) * GR],
                                       WF[:, XLF + c * GR:XLF + (c + 1) * GR], PS[7][:, 0:GR],
                                       op=ALU.mult)
            return inst
        vxln = b.opm("vector", vnb)
        b.wait("scalar", vxln)
        def shx(e):
            inst = None
            for c in range(DCH):
                inst = e.copy(WB[:, XH + c * GR:XH + (c + 1) * GR],
                              WF[:, XLF + c * GR:XLF + (c + 1) * GR])
            return inst
        sxh = b.opm("scalar", shx)
        b.wait("vector", sxh)
        def vlx(e):
            inst = None
            for c in range(DCH):
                inst = e.tensor_tensor(WB[:, XL + c * GR:XL + (c + 1) * GR],
                                       WF[:, XLF + c * GR:XLF + (c + 1) * GR],
                                       WB[:, XH + c * GR:XH + (c + 1) * GR], op=ALU.subtract)
            return inst
        vspl = b.opm("vector", vlx)
        for k in range(DCH):
            par = k % 2
            whi = WB[:, WHI + par * 2048:WHI + par * 2048 + 1024]
            wlo = WB[:, WHI + par * 2048 + 1024:WHI + par * 2048 + 2048]
            if lastw[par] is not None:
                b.wait("sync", lastw[par])
            if gr == 0:
                wsp = WF[:, WSP:WSP + 1024]
                b.wait("sync", lastsp)
                if lastvm is not None:
                    b.wait("sync", lastvm)
                tok = b.dma("sync", wsp, g["cWq_h"][k * 128:(k + 1) * 128, :], sem="qW0")
                b.wait("scalar", tok)
                if sple[par] is not None:
                    b.wait("scalar", sple[par])
                def shw(e, wsp=wsp, whi=whi):
                    return e.copy(whi, wsp)
                lastsp = b.opm("scalar", shw)
                b.wait("vector", lastsp)
                def vlw(e, wsp=wsp, whi=whi, wlo=wlo):
                    return e.tensor_tensor(wlo, wsp, whi, op=ALU.subtract)
                vmw = b.opm("vector", vlw)
                lastvm = vmw
                b.wait("gpsimd", vmw)
                sple[par] = b.dma("gpsimd", g["cwq_scr"][k * 128:(k + 1) * 128, :],
                                  WB[:, WHI + par * 2048:WHI + par * 2048 + 2048],
                                  sem="qX%d" % par)
                b.wait("tensor", vmw, vspl)
            else:
                if sple[par] is not None:
                    b.wait("sync", sple[par])
                    sple[par] = None
                tok = b.dma("sync", WB[:, WHI + par * 2048:WHI + par * 2048 + 2048],
                            g["cwq_scr"][k * 128:(k + 1) * 128, :], sem="qW%d" % par)
                b.wait("tensor", tok, vspl)
            def mq2(e, k=k, whi=whi, wlo=wlo):
                xhs = WB[:, XH + k * GR:XH + (k + 1) * GR]
                xls = WB[:, XL + k * GR:XL + (k + 1) * GR]
                inst = None
                for mi in range(8):
                    wch = whi[:, mi * 128:(mi + 1) * 128]
                    wcl = wlo[:, mi * 128:(mi + 1) * 128]
                    e.matmul(PS[mi][:, 0:GR], wch, xhs, start=(k == 0), stop=False)
                    e.matmul(PS[mi][:, 0:GR], wch, xls, start=False, stop=False)
                    inst = e.matmul(PS[mi][:, 0:GR], wcl, xhs, start=False, stop=(k == DCH - 1))
                return inst
            lastw[par] = b.opm("tensor", mq2)
        tq2 = lastw[(DCH - 1) % 2]
        b.wait("vector", tq2)
        def sq2e(e):
            inst = None
            for mi in range(8):
                inst = e.tensor_scalar(WF[:, QH2 + mi * GR:QH2 + (mi + 1) * GR], PS[mi][:, 0:GR],
                                       0.125, C_BQ[:, mi:mi + 1], op0=ALU.mult, op1=ALU.add)
            return inst
        sq2m = b.opm("vector", sq2e)
        b.wait("tensor", sq2m)
        for c in range(NCH):
            b.wait("tensor", ("eV", b.cnt["eV"]), ("eS", b.cnt["eS"]))
            def msc(e, c=c):
                inst = None
                for p in range(8):
                    inst = e.matmul(PS[c][0:128, p * 64:(p + 1) * 64],
                                    WF[:, QH2 + p * GR + c * 128:QH2 + p * GR + (c + 1) * 128],
                                    WF[:, K2B + p * 64:K2B + (p + 1) * 64],
                                    start=True, stop=True)
                return inst
            smk = b.opm("tensor", msc)
            b.wait("vector", smk)
            def vmx(e, c=c):
                return e.tensor_reduce(WF[:, SMX:SMX + 32],
                                       PS[c][:, 0:512].rearrange("p (a b) -> p a b", b=16),
                                       axis=AX.X, op=ALU.max, negate=True)
            v1 = b.opm("vector", vmx)
            b.wait("vector", v1)
            def vsub(e, c=c):
                return e.tensor_tensor(
                    WF[:, P2B:P2B + 512].rearrange("p (a b) -> p a b", b=16),
                    PS[c][:, 0:512].rearrange("p (a b) -> p a b", b=16),
                    WF[:, SMX:SMX + 32].broadcast_to([128, 32, 16]), op=ALU.add)
            v2 = b.opm("vector", vsub)
            b.wait("scalar", v2)
            def sxp(e):
                return e.activation(WF[:, P2B:P2B + 512], WF[:, P2B:P2B + 512], AF.Exp)
            s1 = b.opm("scalar", sxp)
            b.wait("vector", s1)
            def vsm(e):
                return e.tensor_reduce(WF[:, SSM:SSM + 32],
                                       WF[:, P2B:P2B + 512].rearrange("p (a b) -> p a b", b=16),
                                       axis=AX.X, op=ALU.add)
            v3 = b.opm("vector", vsm)
            b.wait("vector", v3)
            def vrc(e):
                return e.reciprocal(WF[:, SSM:SSM + 32], WF[:, SSM:SSM + 32])
            v4 = b.opm("vector", vrc)
            b.wait("vector", v4)
            def vpn(e):
                return e.tensor_tensor(
                    WF[:, P2B:P2B + 512].rearrange("p (a b) -> p a b", b=16),
                    WF[:, P2B:P2B + 512].rearrange("p (a b) -> p a b", b=16),
                    WF[:, SSM:SSM + 32].broadcast_to([128, 32, 16]), op=ALU.mult)
            ptm = b.opm("vector", vpn)
            b.wait("tensor", ptm)
            if lastavk is not None:
                b.wait("tensor", lastavk)
            if lastpte is not None:
                b.wait("tensor", lastpte)
            def mtp(e):
                inst = None
                for p in range(8):
                    inst = e.transpose(PS[2 + p // 4][0:64, (p % 4) * 128:(p % 4) * 128 + 128],
                                       WF[:, P2B + p * 64:P2B + (p + 1) * 64], IDENT)
                return inst
            tpm = b.opm("tensor", mtp)
            b.wait("vector", tpm)
            def vpte(e):
                e.tensor_copy(WF[0:64, PTS:PTS + 512], PS[2][0:64, 0:512])
                return e.tensor_copy(WF[0:64, PTS + 512:PTS + 1024], PS[3][0:64, 0:512])
            lastpte = b.opm("vector", vpte)
            b.wait("tensor", lastpte, vv32m)
            if avprev[c] is not None:
                b.wait("tensor", avprev[c])
            def mav2(e, c=c):
                inst = None
                for p in range(8):
                    bank = 4 + 2 * c + p // 4
                    colo = (p % 4) * 128
                    e.matmul(PS[bank][0:64, colo:colo + 128],
                             WF[0:16, VH2 + (2 * p) * 64:VH2 + (2 * p + 1) * 64],
                             WF[0:16, PTS + p * 128:PTS + (p + 1) * 128],
                             start=True, stop=True)
                    inst = e.matmul(PS[bank][64:128, colo:colo + 128],
                                    WF[32:48, VH2 + (2 * p + 1) * 64:VH2 + (2 * p + 2) * 64],
                                    WF[32:48, PTS + p * 128:PTS + (p + 1) * 128],
                                    start=True, stop=True)
                return inst
            lastavk = b.opm("tensor", mav2)
            b.wait("vector", lastavk)
            def vavf(e, c=c):
                inst = None
                for bi in range(2):
                    bank = 4 + 2 * c + bi
                    base = AVF + bi * 1024 + c * 128
                    inst = e.tensor_copy(
                        WF[:, base:base + 1024].rearrange("p (a b) -> p a b", b=256)[:, :, 0:128],
                        PS[bank][:, 0:512].rearrange("p (a b) -> p a b", b=128))
                return inst
            avprev[c] = b.opm("vector", vavf)
        for k in range(DCH):
            wsp = WF[:, WSP:WSP + 1024]
            b.wait("sync", tq2 if k == 0 else lastw[0])
            tok = b.dma("sync", wsp, g["cWo_h"][k * 128:(k + 1) * 128, :], sem="qW0")
            b.wait("tensor", tok, avprev[0], avprev[1])
            def mo2(e, k=k, wsp=wsp):
                inst = None
                for mi in range(8):
                    inst = e.matmul(PS[mi][:, 0:GR], wsp[:, mi * 128:(mi + 1) * 128],
                                    WF[:, AVF + k * 256:AVF + (k + 1) * 256],
                                    start=(k == 0), stop=(k == DCH - 1))
                return inst
            lastw[0] = b.opm("tensor", mo2)
        tmo = lastw[0]
        b.wait("vector", tmo)
        def vres(e, ts=ts):
            inst = None
            for mi in range(8):
                inst = e.scalar_tensor_tensor(X[:, mi * T + ts:mi * T + ts + GR],
                                              PS[mi][:, 0:GR], C_BO[:, mi:mi + 1],
                                              X[:, mi * T + ts:mi * T + ts + GR],
                                              op0=ALU.add, op1=ALU.add)
            return inst
        vrm = b.opm("vector", vres)
        b.wait("tensor", vrm)
